# version 13
# baseline (speedup 1.0000x reference)
import numpy as np
import ml_dtypes
import concourse.bass as bass
import concourse.mybir as mybir
from concourse.bass_utils import run_bass_kernel_spmd
from concourse.alu_op_type import AluOpType as OP
from contextlib import ExitStack

F32 = mybir.dt.float32
BF16 = mybir.dt.bfloat16
AF = mybir.ActivationFunctionType
AX = mybir.AxisListType

D = 1024
L = 4096
TMY = 2048
NKC = 32
NQC = 4
QW = 512
NB = 4
C_SHIFT = 12.0
NTILE = 32
EPS = 1e-5


class Sem:
    __slots__ = ("h", "n")

    def __init__(self, h):
        self.h = h
        self.n = 0


def _band_pairs():
    pairs = []
    for qc in range(4):
        for k in range(max(0, 4 * qc - 1), min(15, 4 * qc + 4) + 1):
            pairs.append((qc, k, "c", 512 * qc - 128 * k + 128))
    pairs.append((3, 16, "0", 512 * 3 - 128 * 16 + 128))
    pairs.append((0, 31, "1", 256))
    out = {}
    for qc, k, tb, c0 in pairs:
        j_lo = max(0, -c0)
        j_hi = min(QW, 384 - c0)
        assert j_hi > j_lo, (qc, k, tb, c0)
        out.setdefault((qc, k), []).append((tb, c0, j_lo, j_hi))
    return out


BANDS = _band_pairs()


def build():
    nc = bass.Bass("TRN2", target_bir_lowering=False)
    xd = nc.dram_tensor("x", [L, D], F32, kind="ExternalInput").ap()
    wqT = nc.dram_tensor("wqT", [D, D], F32, kind="ExternalInput").ap()
    wkT = nc.dram_tensor("wkT", [D, D], F32, kind="ExternalInput").ap()
    wvT = nc.dram_tensor("wvT", [D, D], F32, kind="ExternalInput").ap()
    woT = nc.dram_tensor("woT", [D, D], F32, kind="ExternalInput").ap()
    bqd = nc.dram_tensor("bq", [D], F32, kind="ExternalInput").ap()
    bkd = nc.dram_tensor("bk", [D], F32, kind="ExternalInput").ap()
    bvd = nc.dram_tensor("bv", [D], F32, kind="ExternalInput").ap()
    bod = nc.dram_tensor("bo", [D], F32, kind="ExternalInput").ap()
    lamd = nc.dram_tensor("lam", [1, 1], F32, kind="ExternalInput").ap()
    fkd = nc.dram_tensor("fk", [D, D], F32, kind="ExternalInput").ap()
    fbd = nc.dram_tensor("fb", [D], F32, kind="ExternalInput").ap()
    fad = nc.dram_tensor("fa", [D, 16], F32, kind="ExternalInput").ap()
    fbm = nc.dram_tensor("fbm", [16, D], F32, kind="ExternalInput").ap()
    dlc = nc.dram_tensor("dlc", [128, 384], F32, kind="ExternalInput").ap()
    dl0 = nc.dram_tensor("dl0", [128, 384], F32, kind="ExternalInput").ap()
    dl1 = nc.dram_tensor("dl1", [128, 384], F32, kind="ExternalInput").ap()
    e2d = nc.dram_tensor("e2", [2, 128], BF16, kind="ExternalInput").ap()
    idd = nc.dram_tensor("identf", [128, 128], F32, kind="ExternalInput").ap()
    idbd = nc.dram_tensor("identb", [128, 128], BF16, kind="ExternalInput").ap()
    xnh_s = nc.dram_tensor("xnh_s", [8, 128, L], BF16).ap()
    xnl_s = nc.dram_tensor("xnl_s", [8, 128, L], BF16).ap()
    aos = nc.dram_tensor("aos", [8, 128, TMY], F32).ap()
    yd = nc.dram_tensor("y", [TMY, D], F32, kind="ExternalOutput").ap()

    st_all = ExitStack()
    e = st_all.enter_context
    dL = Sem(e(nc.semaphore("dL")))
    dG = Sem(e(nc.semaphore("dG")))
    dO = Sem(e(nc.semaphore("dO")))
    sT = Sem(e(nc.semaphore("sT")))
    sV = Sem(e(nc.semaphore("sV")))
    sS = Sem(e(nc.semaphore("sS")))
    sW = Sem(e(nc.semaphore("sW")))
    sWs = Sem(e(nc.semaphore("sWs")))
    sG = Sem(e(nc.semaphore("sG")))

    def fence(eng, instr, sem=None):
        s_ = sem if sem is not None else sW
        instr.then_inc(s_.h, 1)
        s_.n += 1
        eng.wait_ge(s_.h, s_.n)

    psS2 = e(nc.psum_tensor("psS2", [128, 2048], F32))
    psA = psS2[:, 0:1024]
    psB = psS2[:, 1024:2048]
    psO = e(nc.psum_tensor("psO", [128, 1024], F32))
    psU = e(nc.psum_tensor("psU", [128, 512], F32))
    psV = e(nc.psum_tensor("psV", [128, 512], F32))

    identf = e(nc.sbuf_tensor("sb_idf", [128, 128], F32))
    identb = e(nc.sbuf_tensor("sb_idb", [128, 128], BF16))
    e2 = e(nc.sbuf_tensor("sb_e2", [2, 128], BF16))
    onesb = e(nc.sbuf_tensor("onesb", [128, 1], BF16))
    onesf = e(nc.sbuf_tensor("onesf", [1, 128], F32))
    negc = e(nc.sbuf_tensor("negc", [128, 1], F32))
    tbl = {t: e(nc.sbuf_tensor(f"tbl{t}", [128, 384], F32)) for t in "c01"}
    lamc = e(nc.sbuf_tensor("lamc", [128, 1], F32))
    bqc = e(nc.sbuf_tensor("bqc", [128, 8], F32))
    bkc = e(nc.sbuf_tensor("bkc", [128, 8], F32))
    bvb = e(nc.sbuf_tensor("bvb", [128, D], F32))
    rowt = e(nc.sbuf_tensor("rowt", [1, D], F32))
    sevens = e(nc.sbuf_tensor("sevens", [1, 1], F32))
    r7 = e(nc.sbuf_tensor("r7", [1, 1], F32))

    block = e(nc.Block())
    ev_v_last = [0]

    @block.sync
    def _(sy):
        for dst, src in [(identf, idd), (identb, idbd), (e2, e2d),
                         (tbl["c"], dlc), (tbl["0"], dl0), (tbl["1"], dl1)]:
            sy.dma_start(dst[:], src).then_inc(dL.h, 16)
        sy.dma_start(bqc[:], bqd.rearrange("(c p) -> p c", p=128), allow_slow_non_contiguous=True).then_inc(dL.h, 16)
        sy.dma_start(bkc[:], bkd.rearrange("(c p) -> p c", p=128), allow_slow_non_contiguous=True).then_inc(dL.h, 16)
        sy.dma_start(rowt[:], bvd.rearrange("(o d) -> o d", o=1)).then_inc(dL.h, 16)
        sy.dma_start(lamc[0:1, 0:1], lamd).then_inc(dL.h, 16)
    dL.n += 10 * 16

    @block.gpsimd
    def _(g):
        g.memset(onesb[:], 1.0)
        g.memset(onesf[:], 1.0)
        g.memset(sevens[:], 7.0)
        g.memset(negc[:], -C_SHIFT).then_inc(sG.h, 1)
    sG.n += 1

    @block.tensor
    def _(t):
        t.wait_ge(dL.h, dL.n)
        t.wait_ge(sG.h, sG.n)
        t.matmul(psU[0:128, 0:1], onesf[:], lamc[0:1, 0:1], start=True,
                 stop=True, skip_group_check=True)
        t.matmul(psA[:, 0:512], onesf[:], rowt[0:1, 0:512], start=True,
                 stop=True, skip_group_check=True)
        t.matmul(psA[:, 512:1024], onesf[:], rowt[0:1, 512:1024], start=True,
                 stop=True, skip_group_check=True).then_inc(sT.h, 1)
    sT.n += 1

    @block.vector
    def _(v):
        v.wait_ge(sT.h, sT.n)
        v.tensor_copy(lamc[:], psU[0:128, 0:1])
        i = v.tensor_copy(bvb[:], psA[:])
        fence(v, i)
        v.reciprocal(r7[:], sevens[:]).then_inc(sV.h, 1)
    sV.n += 1

    msk1 = e(nc.sbuf_tensor("msk1", [128, 384], F32))
    msk2 = e(nc.sbuf_tensor("msk2", [128, 384], F32))
    for tname in "c01":
        tb = tbl[tname]

        @block.vector
        def _(v, tb=tb):
            v.wait_ge(sV.h, sV.n)
            i = v.scalar_tensor_tensor(msk1[:], tb[:], -1.0, tb[:],
                                       op0=OP.mult, op1=OP.max)
            fence(v, i)
            v.tensor_scalar_add(msk2[:], msk1[:], -30.0)
            v.tensor_scalar_add(msk1[:], msk1[:], -1.0).then_inc(sV.h, 1)
        sV.n += 1

        @block.scalar
        def _(s):
            s.wait_ge(sV.h, sV.n)
            i = s.activation(msk1[:], msk1[:], AF.Square, bias=0.0, scale=1.0)
            i = s.activation(msk2[:], msk2[:], AF.Square, bias=0.0, scale=1.0)
            fence(s, i, sWs)
            s.activation(msk1[:], msk1[:], AF.Exp, bias=0.0, scale=-2.0)
            s.activation(msk2[:], msk2[:], AF.Exp, bias=0.0,
                         scale=-0.125).then_inc(sS.h, 1)
        sS.n += 1

        @block.vector
        def _(v, tb=tb):
            v.wait_ge(sS.h, sS.n)
            i = v.tensor_scalar_mul(msk1[:], msk1[:], 0.8)
            fence(v, i)
            i = v.scalar_tensor_tensor(tb[:], msk2[:], 0.4, msk1[:],
                                       op0=OP.mult, op1=OP.add)
            fence(v, i)
            v.tensor_scalar_mul(tb[:], tb[:], lamc[:, 0:1]).then_inc(sV.h, 1)
        sV.n += 1

    _sid_ln = nc.enter_named_scope("lnphase", False)[0]
    st_p0 = ExitStack()
    e0 = st_p0.enter_context
    xstage = [e0(nc.sbuf_tensor(f"xstage{i}", [128, D], F32)) for i in range(8)]
    xm4 = e0(nc.sbuf_tensor("xm4", [128, 4 * D], F32))
    xnb4 = e0(nc.sbuf_tensor("xnb4", [128, 4 * D], F32))
    xnh4 = e0(nc.sbuf_tensor("xnh4", [128, 4 * D], BF16))
    xnl4 = e0(nc.sbuf_tensor("xnl4", [128, 4 * D], BF16))
    sqd = e0(nc.sbuf_tensor("sqd", [128, D], F32))
    fms_h = e0(nc.sbuf_tensor("fms_h", [128, 8 * 512], BF16))
    fms_l = e0(nc.sbuf_tensor("fms_l", [128, 8 * 512], BF16))
    st4 = [e0(nc.sbuf_tensor(f"st4_{i}", [128, 4], F32)) for i in range(6)]

    ev_v = [0] * 8
    scw_g = [0] * 8
    ln_v = [0] * 8
    trp_t = [0] * 8
    ln_s = [0] * 8
    for g in range(8):
        @block.sync
        def _(sy, g=g):
            for i in range(4):
                ti = 4 * g + i
                if ti >= 8:
                    sy.wait_ge(sV.h, ln_v[(ti - 8) // 4])
                sy.dma_start(xstage[ti % 8][:],
                             xd[128 * ti:128 * ti + 128, :]).then_inc(dL.h, 16)
        dL.n += 4 * 16

        @block.vector
        def _(v, g=g):
            v.wait_ge(dL.h, dL.n)
            if g >= 1:
                v.wait_ge(sS.h, ln_s[g - 1])
            for i_ in range(4):
                i = v.reduce_sum(st4[0][:, i_:i_ + 1], xstage[(4 * g + i_) % 8][:],
                                 axis=AX.X)
                fence(v, i)
            i = v.tensor_scalar_mul(st4[1][:], st4[0][:], -1.0 / 1024.0)
            fence(v, i)
            for i_ in range(4):
                i = v.tensor_scalar_add(xm4[:, 1024 * i_:1024 * i_ + 1024],
                                        xstage[(4 * g + i_) % 8][:],
                                        st4[1][:, i_:i_ + 1])
            i.then_inc(sV.h, 1)
        sV.n += 1

        @block.scalar
        def _(s, g=g):
            s.wait_ge(sV.h, sV.n)
            i = None
            for i_ in range(4):
                i = s.activation(sqd[:], xm4[:, 1024 * i_:1024 * i_ + 1024],
                                 AF.Square, bias=0.0, scale=1.0,
                                 accum_out=st4[2][:, i_:i_ + 1])
            i.then_inc(sS.h, 1)
        sS.n += 1

        @block.vector
        def _(v):
            v.wait_ge(sS.h, sS.n)
            v.tensor_scalar(st4[3][:], st4[2][:], 1.0 / 1024.0, EPS,
                            op0=OP.mult, op1=OP.add).then_inc(sV.h, 1)
        sV.n += 1

        @block.scalar
        def _(s):
            s.wait_ge(sV.h, sV.n)
            s.activation(st4[4][:], st4[3][:], AF.Sqrt, bias=0.0,
                         scale=1.0).then_inc(sS.h, 1)
        sS.n += 1

        @block.vector
        def _(v, g=g):
            v.wait_ge(sS.h, sS.n)
            if g >= 1:
                v.wait_ge(sT.h, trp_t[g - 1])
            i = v.reciprocal(st4[5][:], st4[4][:])
            fence(v, i)
            i = v.tensor_mul(st4[0][:], st4[5][:], st4[5][:])
            fence(v, i)
            i = v.tensor_mul(st4[1][:], st4[0][:], st4[3][:])
            fence(v, i)
            i = v.tensor_scalar(st4[0][:], st4[1][:], -0.5, 1.5,
                                op0=OP.mult, op1=OP.add)
            fence(v, i)
            i = v.tensor_mul(st4[1][:], st4[5][:], st4[0][:])
            fence(v, i)
            for i_ in range(4):
                i = v.tensor_scalar_mul(xnb4[:, 1024 * i_:1024 * i_ + 1024],
                                        xm4[:, 1024 * i_:1024 * i_ + 1024],
                                        st4[1][:, i_:i_ + 1])
            fence(v, i)
            i = v.tensor_copy(xnh4[:], xnb4[:])
            fence(v, i)
            v.tensor_tensor(xnl4[:], xnb4[:], xnh4[:],
                            op=OP.subtract).then_inc(sV.h, 1)
        sV.n += 1
        ln_v[g] = sV.n
        ln_s[g] = sS.n

        for i_ in range(4):
            @block.tensor
            def _(t, g=g, i_=i_):
                t.wait_ge(sV.h, ln_v[g])
                if not (g == 0 and i_ == 0):
                    t.wait_ge(sV.h, ev_v_last[0])
                last = None
                for c in range(8):
                    last = t.matmul(psA[:, 64 * c:64 * c + 64].bitcast(BF16),
                                    xnh4[:, 1024 * i_ + 128 * c:
                                         1024 * i_ + 128 * c + 128],
                                    identb[:], is_transpose=True, start=True,
                                    stop=True, skip_group_check=True)
                    last.then_inc(sT.h, 1)
                    last = t.matmul(psB[:, 64 * c:64 * c + 64].bitcast(BF16),
                                    xnl4[:, 1024 * i_ + 128 * c:
                                         1024 * i_ + 128 * c + 128],
                                    identb[:], is_transpose=True, start=True,
                                    stop=True, skip_group_check=True)
                    last.then_inc(sT.h, 1)
            sT.n += 16

            @block.vector
            def _(v, g=g, i_=i_):
                v.wait_ge(sT.h, sT.n)
                if g >= 1 and i_ == 0:
                    v.wait_ge(dG.h, scw_g[g - 1])
                fmt_h = fms_h.tensor if hasattr(fms_h, "tensor") else fms_h
                fmt_l = fms_l.tensor if hasattr(fms_l, "tensor") else fms_l
                dsth = bass.AP(fmt_h, 128 * i_, [[4096, 128], [512, 8], [1, 128]])
                dstl = bass.AP(fmt_l, 128 * i_, [[4096, 128], [512, 8], [1, 128]])
                v.tensor_copy(dsth, psA[:, 0:512].bitcast(BF16))
                i = v.tensor_copy(dstl, psB[:, 0:512].bitcast(BF16))
                i.then_inc(sV.h, 1)
            sV.n += 1
            ev_v_last[0] = sV.n
        ev_v[g] = sV.n
        trp_t[g] = sT.n

        @block.gpsimd
        def _(g_, g=g):
            g_.wait_ge(sV.h, ev_v[g])
            for c in range(8):
                g_.dma_start(xnh_s[c, :, 512 * g:512 * g + 512],
                             fms_h[:, 512 * c:512 * c + 512]).then_inc(dG.h, 16)
                g_.dma_start(xnl_s[c, :, 512 * g:512 * g + 512],
                             fms_l[:, 512 * c:512 * c + 512]).then_inc(dG.h, 16)
        dG.n += 16 * 16
        scw_g[g] = dG.n

    st_p0.close()
    st_p1 = ExitStack()
    e1 = st_p1.enter_context
    khi = e1(nc.sbuf_tensor("khi", [128, 2 * L], BF16))
    klo = e1(nc.sbuf_tensor("klo", [128, 2 * L], BF16))
    vhi = e1(nc.sbuf_tensor("vhi", [128, 32 * 256], BF16))
    vlo = e1(nc.sbuf_tensor("vlo", [128, 32 * 256], BF16))
    qhi = e1(nc.sbuf_tensor("qhi", [128, 2 * TMY], BF16))
    qlo = e1(nc.sbuf_tensor("qlo", [128, 2 * TMY], BF16))
    wst = None
    wsp = {}
    for wn in ("q", "k", "v"):
        wsp[wn] = (e1(nc.sbuf_tensor(f"w{wn}h", [128, 2048], BF16)),
                   e1(nc.sbuf_tensor(f"w{wn}l", [128, 2048], BF16)))
    fmr_h = e1(nc.sbuf_tensor("fmr_h", [128, 8 * 512], BF16))
    fmr_l = e1(nc.sbuf_tensor("fmr_l", [128, 8 * 512], BF16))
    wst = fmr_h[:, 0:4096].bitcast(F32)
    evr = e1(nc.sbuf_tensor("evr", [128, 1024], F32))
    bvb4 = e1(nc.sbuf_tensor("bvb4", [128, 1024], F32))
    pf2 = [e1(nc.sbuf_tensor(f"pf2{i}", [128, 2048], F32)) for i in range(2)]
    ph2 = [e1(nc.sbuf_tensor(f"ph2{i}", [128, 2048], BF16)) for i in range(2)]
    pl2 = [e1(nc.sbuf_tensor(f"pl2{i}", [128, 2048], BF16)) for i in range(2)]
    aost = [e1(nc.sbuf_tensor(f"aost{i}", [128, 1024], F32)) for i in range(2)]
    z_sb = e1(nc.sbuf_tensor("z_sb", [128, 1024], F32))
    rzb = e1(nc.sbuf_tensor("rzb", [128, 16], F32))
    zsum8 = e1(nc.sbuf_tensor("zsum8", [128, 16], F32))
    zt_sb = e1(nc.sbuf_tensor("zt_sb", [128, 1024], F32))
    rzT = e1(nc.sbuf_tensor("rzT", [2, 1024], F32))
    rzTh = e1(nc.sbuf_tensor("rzTh", [2, 1024], BF16))
    rzTl = e1(nc.sbuf_tensor("rzTl", [2, 1024], BF16))
    b_sb = e1(nc.sbuf_tensor("b_sb", [128, 1024], F32))

    ao_par = 0
    ao_dg = [0, 0]
    norm_end_v = 0
    prj_t_last = [0]
    for hb in range(NB):
        _sid_p = nc.enter_named_scope(f"proj{hb}", False)[0]
        for wi, (wn, wt_d) in enumerate([("q", wqT), ("k", wkT), ("v", wvT)]):
            @block.sync
            def _(sy, wn=wn, wt_d=wt_d, hb=hb, pt=prj_t_last[0], wi=wi,
                  psv=sV.n):
                sy.wait_ge(sV.h, psv)
                if wi == 0 and hb == 0:
                    sy.wait_ge(dG.h, scw_g[7])
                src = bass.AP(wt_d.tensor, 256 * hb,
                              [[1024, 128], [131072, 8], [1, 256]])
                sy.dma_start(wst[:], src).then_inc(dL.h, 16)
            dL.n += 16

            @block.vector
            def _(v, wn=wn):
                v.wait_ge(dL.h, dL.n)
                wh, wl = wsp[wn]
                i = v.tensor_copy(wh[:], wst[:])
                fence(v, i)
                v.tensor_tensor(wl[:], wst[:], wh[:],
                                op=OP.subtract).then_inc(sV.h, 1)
            sV.n += 1

        @block.vector
        def _(v, hb=hb):
            v.wait_ge(sV.h, sV.n)
            for r in range(4):
                i = v.tensor_copy(bvb4[:, 256 * r:256 * r + 256],
                                  bvb[:, 256 * hb:256 * hb + 256])
            i.then_inc(sV.h, 1)
        sV.n += 1
        w_ready_v = sV.n

        prj_t = 0
        prj_v = 0
        for mt in range(8):
            t0 = 512 * mt

            @block.sync
            def _(sy, mt=mt, t0=t0, prj_t=prj_t, hb=hb, wrv=w_ready_v):
                if hb == 0:
                    sy.wait_ge(dG.h, scw_g[7])
                if mt == 0:
                    sy.wait_ge(sV.h, wrv)
                if prj_t:
                    sy.wait_ge(sT.h, prj_t)
                for c in range(8):
                    sy.dma_start(fmr_h[:, 512 * c:512 * c + 512],
                                 xnh_s[c, :, t0:t0 + 512]).then_inc(dL.h, 16)
                    sy.dma_start(fmr_l[:, 512 * c:512 * c + 512],
                                 xnl_s[c, :, t0:t0 + 512]).then_inc(dL.h, 16)
            dL.n += 16 * 16

            @block.tensor
            def _(t, mt=mt, prj_v=prj_v, w_ready_v=w_ready_v,
                  nev=norm_end_v, ss=sS.n):
                t.wait_ge(dL.h, dL.n)
                t.wait_ge(sV.h, max(w_ready_v, prj_v, nev))
                if mt == 0:
                    t.wait_ge(sS.h, ss)
                wqh, wql = wsp["q"]
                wkh, wkl = wsp["k"]
                wvh, wvl = wsp["v"]
                last = None
                for kc in range(2):
                    first = True
                    for c in range(8):
                        for lh, rh in [(wkh, fmr_h), (wkh, fmr_l), (wkl, fmr_h)]:
                            last = t.matmul(
                                psA[:, 512 * kc:512 * kc + 512],
                                lh[:, 256 * c + 128 * kc:256 * c + 128 * kc + 128],
                                rh[:, 512 * c:512 * c + 512],
                                start=first, stop=False, skip_group_check=True)
                            first = False
                for s4 in range(4):
                    first = True
                    for c in range(8):
                        for lh, rh in [(fmr_h, wvh), (fmr_l, wvh), (fmr_h, wvl)]:
                            last = t.matmul(
                                psB[:, 256 * s4:256 * s4 + 256],
                                lh[:, 512 * c + 128 * s4:512 * c + 128 * s4 + 128],
                                rh[:, 256 * c:256 * c + 256],
                                start=first, stop=False, skip_group_check=True)
                            first = False
                if mt < 4:
                    for kc in range(2):
                        first = True
                        for c in range(8):
                            for lh, rh in [(wqh, fmr_h), (wqh, fmr_l), (wql, fmr_h)]:
                                last = t.matmul(
                                    psO[:, 512 * kc:512 * kc + 512],
                                    lh[:, 256 * c + 128 * kc:256 * c + 128 * kc + 128],
                                    rh[:, 512 * c:512 * c + 512],
                                    start=first, stop=False,
                                    skip_group_check=True)
                                first = False
                last.then_inc(sT.h, 1)
            sT.n += 1
            prj_t = sT.n

            @block.vector
            def _(v, mt=mt, t0=t0, hb=hb):
                v.wait_ge(sT.h, sT.n)
                v.tensor_scalar_add(evr[:, 0:512], psA[:, 0:512],
                                    bkc[:, 2 * hb:2 * hb + 1])
                i = v.tensor_scalar_add(evr[:, 512:1024], psA[:, 512:1024],
                                        bkc[:, 2 * hb + 1:2 * hb + 2])
                fence(v, i)
                for kc in range(2):
                    i = v.tensor_copy(khi[:, L * kc + t0:L * kc + t0 + 512],
                                      evr[:, 512 * kc:512 * kc + 512])
                fence(v, i)
                for kc in range(2):
                    i = v.tensor_tensor(klo[:, L * kc + t0:L * kc + t0 + 512],
                                        evr[:, 512 * kc:512 * kc + 512],
                                        khi[:, L * kc + t0:L * kc + t0 + 512],
                                        op=OP.subtract)
                fence(v, i)
                i = v.tensor_tensor(evr[:], psB[:], bvb4[:], op=OP.add)
                fence(v, i)
                i = v.tensor_copy(vhi[:, 1024 * mt:1024 * mt + 1024], evr[:])
                fence(v, i)
                i = v.tensor_tensor(vlo[:, 1024 * mt:1024 * mt + 1024], evr[:],
                                    vhi[:, 1024 * mt:1024 * mt + 1024],
                                    op=OP.subtract)
                if mt < 4:
                    fence(v, i)
                    v.tensor_scalar(evr[:, 0:512], psO[:, 0:512],
                                    bqc[:, 2 * hb:2 * hb + 1], 0.125,
                                    op0=OP.add, op1=OP.mult)
                    i = v.tensor_scalar(evr[:, 512:1024], psO[:, 512:1024],
                                        bqc[:, 2 * hb + 1:2 * hb + 2], 0.125,
                                        op0=OP.add, op1=OP.mult)
                    fence(v, i)
                    for kc in range(2):
                        i = v.tensor_copy(qhi[:, TMY * kc + t0:TMY * kc + t0 + 512],
                                          evr[:, 512 * kc:512 * kc + 512])
                    fence(v, i)
                    for kc in range(2):
                        i = v.tensor_tensor(qlo[:, TMY * kc + t0:TMY * kc + t0 + 512],
                                            evr[:, 512 * kc:512 * kc + 512],
                                            qhi[:, TMY * kc + t0:TMY * kc + t0 + 512],
                                            op=OP.subtract)
                i.then_inc(sV.h, 1)
            sV.n += 1
            prj_v = sV.n

        nc.leave_named_scope(f"proj{hb}", _sid_p, False)
        _sid_a = nc.enter_named_scope(f"attn{hb}", False)[0]
        for qc in range(NQC):
            pg0, pg1 = 2 * hb, 2 * hb + 1
            qb = [TMY * 0 + QW * qc, TMY * 1 + QW * qc]
            entry_s = sS.n
            entry_v = max(sV.n, norm_end_v)
            entry_v0 = sV.n
            entry_t = sT.n
            nb_le = [0] * NKC
            acc = 0
            for k in range(NKC):
                if BANDS.get((qc, k)):
                    acc += 1
                nb_le[k] = acc
            qk_t = [entry_t + (1 if k == 0 else 2 * k) for k in range(NKC)]
            avz_t = [entry_t + (2 * k + 3 if k < NKC - 1 else 2 * NKC)
                     for k in range(NKC)]
            exp_s = [entry_s + k + 1 for k in range(NKC)]
            mask_v = [entry_v0 + nb_le[k] + k if BANDS.get((qc, k)) else 0
                      for k in range(NKC)]
            split_v = [entry_v0 + nb_le[k] + k + 1 for k in range(NKC)]
            V0 = split_v[NKC - 1]
            T0 = avz_t[NKC - 1]

            @block.tensor
            def _(t, qc=qc, qb=qb, prj_v=prj_v, entry_s=entry_s,
                  entry_v=entry_v, exp_s=exp_s, split_v=split_v, V0=V0):
                def qkg(k):
                    if k >= 1:
                        t.wait_ge(sS.h, exp_s[k - 1])
                    else:
                        t.wait_ge(sS.h, entry_s)
                        t.wait_ge(sV.h, prj_v)
                    last = None
                    for pr in range(2):
                        for kb, qx in [(khi, qhi), (khi, qlo), (klo, qhi)]:
                            for h in range(2):
                                ko = L * pr + 128 * k
                                last = t.matmul(
                                    psS2[:, 1024 * pr + 512 * h:
                                         1024 * pr + 512 * h + 512],
                                    kb[64 * h:64 * h + 64, ko:ko + 128],
                                    qx[64 * h:64 * h + 64, qb[pr]:qb[pr] + QW],
                                    start=(kb is khi and qx is qhi),
                                    stop=(kb is klo),
                                    tile_position=(64 * h, 0),
                                    skip_group_check=True)
                    last.then_inc(sT.h, 1)

                def avzg(k):
                    t.wait_ge(sV.h, split_v[k])
                    if k == 0:
                        t.wait_ge(sV.h, entry_v)
                    pi_ = k % 2
                    last = None
                    for pr, zbank in [(0, psU), (1, psV)]:
                        po = 1024 * pr
                        zops = [(ph2[pi_], 0, 0), (pl2[pi_], 32, 0),
                                (ph2[pi_], 64, 1), (pl2[pi_], 96, 1)]
                        for pb, col, h in zops:
                            t.matmul(zbank[col:col + 1, :], onesb[:],
                                     pb[:, po + 512 * h:po + 512 * h + 512],
                                     start=(k == 0), stop=False,
                                     tile_position=(0, col),
                                     skip_group_check=True)
                        for vb, pb in [(vhi, ph2[pi_]), (vlo, ph2[pi_]),
                                       (vhi, pl2[pi_])]:
                            for h in range(2):
                                last = t.matmul(
                                    psO[64 * h:64 * h + 64,
                                        512 * pr:512 * pr + 512],
                                    vb[:, 256 * k + 128 * pr + 64 * h:
                                       256 * k + 128 * pr + 64 * h + 64],
                                    pb[:, po + 512 * h:po + 512 * h + 512],
                                    start=(k == 0 and vb is vhi and pb is ph2[pi_]),
                                    stop=False,
                                    tile_position=(0, 64 * h),
                                    skip_group_check=True)
                    last.then_inc(sT.h, 1)

                qkg(0)
                for k in range(1, NKC):
                    qkg(k)
                    avzg(k - 1)
                avzg(NKC - 1)
                t.wait_ge(sV.h, V0 + 1)
                for j in range(4):
                    t.matmul(psU[:, 128 * j:128 * j + 128],
                             z_sb[:, 128 * j:128 * j + 128], identf[:],
                             is_transpose=True, start=True, stop=True,
                             skip_group_check=True).then_inc(sT.h, 1)
                    t.matmul(psV[:, 128 * j:128 * j + 128],
                             z_sb[:, 512 + 128 * j:512 + 128 * j + 128],
                             identf[:],
                             is_transpose=True, start=True, stop=True,
                             skip_group_check=True).then_inc(sT.h, 1)
                t.wait_ge(sV.h, V0 + 2)
                for J in range(8):
                    bank = psU if J < 4 else psV
                    t.matmul(bank[0:2, 128 * (J % 4):128 * (J % 4) + 128],
                             rzb[:, 2 * J:2 * J + 2], identf[:],
                             is_transpose=True, start=True, stop=True,
                             skip_group_check=True).then_inc(sT.h, 1)
                t.wait_ge(sV.h, V0 + 3)
                for pr, bank in [(0, psU), (1, psV)]:
                    t.matmul(bank[:], e2[:], rzTh[:, 512 * pr:512 * pr + 512],
                             start=True, stop=False, skip_group_check=True)
                    t.matmul(bank[:], e2[:], rzTl[:, 512 * pr:512 * pr + 512],
                             start=False, stop=True,
                             skip_group_check=True).then_inc(sT.h, 1)
            sT.n = T0 + 16 + 2
            T1 = sT.n

            @block.vector
            def _(v, qc=qc, exp_s=exp_s, avz_t=avz_t, qk_t=qk_t,
                  T0=T0, ao_par=ao_par):
                for k in range(NKC):
                    pi = k % 2
                    bands = BANDS.get((qc, k), [])
                    if bands:
                        v.wait_ge(sT.h, qk_t[k])
                        i = None
                        for tn, c0, j_lo, j_hi in bands:
                            for pr in range(2):
                                for h in range(2):
                                    o = 1024 * pr + 512 * h
                                    i = v.tensor_tensor(
                                        psS2[:, o + j_lo:o + j_hi],
                                        psS2[:, o + j_lo:o + j_hi],
                                        tbl[tn][:, j_lo + c0:j_hi + c0],
                                        op=OP.add)
                        i.then_inc(sV.h, 1)
                    v.wait_ge(sS.h, exp_s[k])
                    if k >= 2:
                        v.wait_ge(sT.h, avz_t[k - 2])
                    v.tensor_tensor(pl2[pi][:], pf2[pi][:], ph2[pi][:],
                                    op=OP.subtract).then_inc(sV.h, 1)
                v.wait_ge(sT.h, avz_t[NKC - 1])
                v.tensor_copy(z_sb[:, 0:512], psU[:])
                v.tensor_copy(z_sb[:, 512:1024], psV[:]).then_inc(sV.h, 1)
                v.wait_ge(sT.h, T0 + 8)
                v.tensor_copy(zt_sb[:, 0:512], psU[:])
                i = v.tensor_copy(zt_sb[:, 512:1024], psV[:])
                fence(v, i)
                ztt = zt_sb.tensor if hasattr(zt_sb, "tensor") else zt_sb
                va = bass.AP(ztt, 0, [[1024, 128], [128, 8], [64, 2]])
                vb2 = bass.AP(ztt, 32, [[1024, 128], [128, 8], [64, 2]])
                i = v.tensor_tensor(zsum8[:, 0:16], va, vb2, op=OP.add)
                fence(v, i)
                v.reciprocal(rzb[:, 0:16], zsum8[:, 0:16]).then_inc(sV.h, 1)
                v.wait_ge(sT.h, T0 + 16)
                v.tensor_copy(rzT[:, 0:512], psU[0:2, :])
                i = v.tensor_copy(rzT[:, 512:1024], psV[0:2, :])
                fence(v, i)
                i = v.tensor_copy(rzTh[:], rzT[:])
                fence(v, i)
                v.tensor_tensor(rzTl[:], rzT[:], rzTh[:],
                                op=OP.subtract).then_inc(sV.h, 1)
                v.wait_ge(sT.h, T0 + 18)
                v.wait_ge(dG.h, ao_dg[ao_par])
                v.tensor_copy(b_sb[:, 0:512], psU[:])
                i = v.tensor_copy(b_sb[:, 512:1024], psV[:])
                fence(v, i)
                v.tensor_tensor(aost[ao_par][:, 0:512],
                                psO[:, 0:512], b_sb[:, 0:512], op=OP.mult)
                v.tensor_tensor(aost[ao_par][:, 512:1024],
                                psO[:, 512:1024], b_sb[:, 512:1024],
                                op=OP.mult).then_inc(sV.h, 1)
            sV.n = V0 + 4
            norm_end_v = sV.n

            @block.scalar
            def _(s, qc=qc, qk_t=qk_t, mask_v=mask_v, split_v=split_v):
                for k in range(NKC):
                    pi = k % 2
                    s.wait_ge(sT.h, qk_t[k])
                    if mask_v[k]:
                        s.wait_ge(sV.h, mask_v[k])
                    if k >= 2:
                        s.wait_ge(sV.h, split_v[k - 2])
                        s.wait_ge(sT.h, avz_t[k - 2])
                    i = s.activation(pf2[pi][:], psS2[:], AF.Exp,
                                     bias=negc[:], scale=1.0)
                    fence(s, i, sWs)
                    s.copy(ph2[pi][:], pf2[pi][:]).then_inc(sS.h, 1)
            sS.n += NKC

            @block.gpsimd
            def _(g, ao_par=ao_par, pg0=pg0, pg1=pg1, qc=qc):
                g.wait_ge(sV.h, sV.n)
                g.dma_start(aos[pg0, :, QW * qc:QW * qc + QW],
                            aost[ao_par][:, 0:512]).then_inc(dG.h, 16)
                g.dma_start(aos[pg1, :, QW * qc:QW * qc + QW],
                            aost[ao_par][:, 512:1024]).then_inc(dG.h, 16)
            dG.n += 32
            ao_dg[ao_par] = dG.n
            ao_par ^= 1

        nc.leave_named_scope(f"attn{hb}", _sid_a, False)
    st_p1.close()
    _sid_p2 = nc.enter_named_scope("p2", False)[0]
    st_p2 = ExitStack()
    e2s = st_p2.enter_context
    woTs = e2s(nc.sbuf_tensor("woTs", [128, 8 * 1024], F32))
    wfst = e2s(nc.sbuf_tensor("wfst", [128, 8 * 1024], F32))
    wfh = e2s(nc.sbuf_tensor("wfh", [128, 8 * 1024], BF16))
    wfl = e2s(nc.sbuf_tensor("wfl", [128, 8 * 1024], BF16))
    aor = e2s(nc.sbuf_tensor("aor", [128, 8 * 512], F32))
    fm2_h = e2s(nc.sbuf_tensor("fm2_h", [128, 8 * 512], BF16))
    fm2_l = e2s(nc.sbuf_tensor("fm2_l", [128, 8 * 512], BF16))
    xres = [e2s(nc.sbuf_tensor(f"xres{i}", [128, D], F32)) for i in range(4)]
    outst = [e2s(nc.sbuf_tensor(f"outst{i}", [128, D], F32)) for i in range(2)]
    ybuf = e2s(nc.sbuf_tensor("ybuf", [128, D], F32))
    gbuf = e2s(nc.sbuf_tensor("gbuf", [128, D], F32))
    sbuf2 = e2s(nc.sbuf_tensor("sbuf2", [128, D], F32))
    bo_b = e2s(nc.sbuf_tensor("bo_b", [128, D], F32))
    fb_b = e2s(nc.sbuf_tensor("fb_b", [128, D], F32))
    fm2B_h = wfst[:, 0:2048].bitcast(BF16)
    fm2B_l = wfst[:, 2048:4096].bitcast(BF16)
    aorB = wfst[:, 4096:8192]
    qt1 = aor[:, 0:1024]
    qt2 = aor[:, 1024:2048]
    qt3 = aor[:, 2048:3072]
    mk8 = e2s(nc.sbuf_tensor("mk8", [128, 8], F32))
    mcol = e2s(nc.sbuf_tensor("mcol", [128, 1], F32))
    mrow = e2s(nc.sbuf_tensor("mrow", [1, 128], F32))
    mm1 = e2s(nc.sbuf_tensor("mm1", [1, 1], F32))
    scA = e2s(nc.sbuf_tensor("scA", [1, 1], F32))
    invA = e2s(nc.sbuf_tensor("invA", [1, 1], F32))
    sic = e2s(nc.sbuf_tensor("sic", [128, 2], F32))
    afh = e2s(nc.sbuf_tensor("afh", [128, 128], BF16))
    afl = e2s(nc.sbuf_tensor("afl", [128, 128], BF16))
    afst = e2s(nc.sbuf_tensor("afst", [128, 128], F32))
    bmst = e2s(nc.sbuf_tensor("bmst", [16, D], F32))
    bmh = e2s(nc.sbuf_tensor("bmh", [16, D], BF16))
    bml = e2s(nc.sbuf_tensor("bml", [16, D], BF16))
    rfb = e2s(nc.sbuf_tensor("rfb", [16, 512], F32))
    rTh = e2s(nc.sbuf_tensor("rTh", [16, 512], BF16))
    rTl = e2s(nc.sbuf_tensor("rTl", [16, 512], BF16))

    @block.sync
    def _(sy):
        sy.wait_ge(sV.h, sV.n)
        sy.dma_start(woTs[:], bass.AP(woT.tensor, 0,
                     [[1024, 128], [131072, 8], [1, 1024]])).then_inc(dL.h, 16)
        sy.dma_start(wfst[:], bass.AP(fkd.tensor, 0,
                     [[1024, 128], [131072, 8], [1, 1024]])).then_inc(dL.h, 16)
        sy.dma_start(afst[:], bass.AP(fad.tensor, 0,
                     [[16, 128], [2048, 8], [1, 16]])).then_inc(dL.h, 16)
        sy.dma_start(bmst[:], fbm).then_inc(dL.h, 16)
        sy.dma_start(rowt[:], bod.rearrange("(o d) -> o d", o=1)).then_inc(dL.h, 16)
    dL.n += 5 * 16

    @block.tensor
    def _(t):
        t.wait_ge(dL.h, dL.n)
        t.matmul(psB[:, 0:512], onesf[:], rowt[0:1, 0:512], start=True,
                 stop=True, skip_group_check=True)
        t.matmul(psB[:, 512:1024], onesf[:], rowt[0:1, 512:1024], start=True,
                 stop=True, skip_group_check=True).then_inc(sT.h, 1)
    sT.n += 1

    @block.vector
    def _(v):
        v.wait_ge(sT.h, sT.n)
        v.tensor_copy(bo_b[:], psB[:]).then_inc(sV.h, 1)
    sV.n += 1

    @block.sync
    def _(sy):
        sy.wait_ge(sV.h, sV.n)
        sy.dma_start(rowt[:], fbd.rearrange("(o d) -> o d", o=1)).then_inc(dL.h, 16)
    dL.n += 16

    @block.tensor
    def _(t):
        t.wait_ge(dL.h, dL.n)
        t.matmul(psB[:, 0:512], onesf[:], rowt[0:1, 0:512], start=True,
                 stop=True, skip_group_check=True)
        t.matmul(psB[:, 512:1024], onesf[:], rowt[0:1, 512:1024], start=True,
                 stop=True, skip_group_check=True).then_inc(sT.h, 1)
    sT.n += 1

    @block.vector
    def _(v):
        v.wait_ge(sT.h, sT.n)
        v.tensor_copy(fb_b[:], psB[:]).then_inc(sV.h, 1)
    sV.n += 1

    @block.vector
    def _(v):
        v.wait_ge(dL.h, dL.n)
        for c in range(8):
            i = v.scalar_tensor_tensor(qt1[:], wfst[:, 1024 * c:1024 * c + 1024],
                                       -1.0, wfst[:, 1024 * c:1024 * c + 1024],
                                       op0=OP.mult, op1=OP.max)
            fence(v, i)
            i = v.reduce_max(mk8[:, c:c + 1], qt1[:], axis=AX.X)
            fence(v, i)
        i = v.reduce_max(mcol[:], mk8[:], axis=AX.X)
        i.then_inc(sV.h, 1)
    sV.n += 1

    @block.tensor
    def _(t):
        t.wait_ge(sV.h, sV.n)
        t.matmul(psU[0:1, 0:128], mcol[:], identf[:], is_transpose=True,
                 start=True, stop=True, skip_group_check=True).then_inc(sT.h, 1)
    sT.n += 1

    @block.vector
    def _(v):
        v.wait_ge(sT.h, sT.n)
        i = v.tensor_copy(mrow[:], psU[0:1, 0:128])
        fence(v, i)
        i = v.reduce_max(mm1[:], mrow[:], axis=AX.X)
        fence(v, i)
        i = v.tensor_scalar_mul(scA[:], mm1[:], r7[0:1, 0:1])
        fence(v, i)
        v.reciprocal(invA[:], scA[:]).then_inc(sV.h, 1)
    sV.n += 1

    @block.tensor
    def _(t):
        t.wait_ge(sV.h, sV.n)
        t.matmul(psU[0:128, 0:1], onesf[:], scA[:], start=True, stop=True,
                 skip_group_check=True)
        t.matmul(psU[0:128, 1:2], onesf[:], invA[:], start=True, stop=True,
                 skip_group_check=True).then_inc(sT.h, 1)
    sT.n += 1

    @block.vector
    def _(v):
        v.wait_ge(sT.h, sT.n)
        i = v.tensor_copy(sic[:], psU[0:128, 0:2])
        fence(v, i)
        for c in range(8):
            w_sl = wfst[:, 1024 * c:1024 * c + 1024]
            i = v.tensor_scalar(qt1[:], w_sl, sic[:, 1:2], 0.5,
                                op0=OP.mult, op1=OP.add)
            fence(v, i)
            i = v.tensor_scalar(qt2[:], qt1[:], -6.0, None, op0=OP.is_ge)
            fence(v, i)
            cur, nxt = qt2, qt3
            for kk in range(-5, 8):
                i = v.scalar_tensor_tensor(nxt[:], qt1[:], float(kk), cur[:],
                                           op0=OP.is_ge, op1=OP.add)
                fence(v, i)
                cur, nxt = nxt, cur
            i = v.tensor_scalar(cur[:], cur[:], 14.0, -7.0,
                                op0=OP.min, op1=OP.add)
            fence(v, i)
            i = v.tensor_scalar_mul(qt1[:], cur[:], sic[:, 0:1])
            fence(v, i)
            i = v.tensor_copy(wfh[:, 1024 * c:1024 * c + 1024], qt1[:])
            fence(v, i)
            i = v.tensor_tensor(wfl[:, 1024 * c:1024 * c + 1024], qt1[:],
                                wfh[:, 1024 * c:1024 * c + 1024],
                                op=OP.subtract)
            fence(v, i)
        i = v.tensor_copy(afh[:], afst[:])
        fence(v, i)
        i = v.tensor_tensor(afl[:], afst[:], afh[:], op=OP.subtract)
        fence(v, i)
        i = v.tensor_copy(bmh[:], bmst[:])
        fence(v, i)
        v.tensor_tensor(bml[:], bmst[:], bmh[:],
                        op=OP.subtract).then_inc(sV.h, 1)
    sV.n += 1
    quant_v = sV.n

    out_tiles = 0
    ep_v = [0] * 20
    rT_v_arr = [0] * 4
    ffn_t = [0] * 20
    sil_s = [0] * 20
    p2t_end = [0] * 4
    for mt in range(4):
        t0 = 512 * mt
        fmPh = fm2_h if mt % 2 == 0 else fm2B_h
        fmPl = fm2_l if mt % 2 == 0 else fm2B_l
        aorP = aor if mt % 2 == 0 else aorB

        @block.sync
        def _(sy, mt=mt, t0=t0, qv=quant_v, fmPh=fmPh, fmPl=fmPl, aorP=aorP):
            if mt == 0:
                sy.wait_ge(sV.h, qv)
            if mt == 1:
                sy.wait_ge(sV.h, qv)
            if mt >= 2:
                sy.wait_ge(sT.h, p2t_end[mt - 2])
            for c in range(8):
                sy.dma_start(fmPh[:, 512 * c:512 * c + 512],
                             xnh_s[c, :, t0:t0 + 512]).then_inc(dL.h, 16)
                sy.dma_start(fmPl[:, 512 * c:512 * c + 512],
                             xnl_s[c, :, t0:t0 + 512]).then_inc(dL.h, 16)
            for pgc in range(8):
                sy.dma_start(aorP[:, 512 * pgc:512 * pgc + 512],
                             aos[pgc, :, t0:t0 + 512]).then_inc(dL.h, 16)
            for tt in range(4):
                if mt >= 1:
                    sy.wait_ge(sV.h, ep_v[4 * (mt - 1) + tt])
                sy.dma_start(xres[tt][:],
                             xd[t0 + 128 * tt:t0 + 128 * tt + 128, :]
                             ).then_inc(dL.h, 16)
        dL.n += 28 * 16
        ld_d = dL.n
        sV_entry = sV.n
        sS_entry = sS.n
        rT_v_arr[mt] = sV_entry + 1
        for tt in range(4):
            ep_v[4 * mt + tt] = sV_entry + 3 + 2 * tt
            sil_s[4 * mt + tt] = sS_entry + tt + 1

        @block.tensor
        def _(t, mt=mt, ld_d=ld_d, quant_v=quant_v, fmPh=fmPh, fmPl=fmPl,
              aorP=aorP):
            fm2_h, fm2_l, aor = fmPh, fmPl, aorP
            t.wait_ge(dL.h, ld_d)
            t.wait_ge(sV.h, quant_v)
            if mt >= 1:
                t.wait_ge(sV.h, ep_v[4 * mt - 1])
            first = True
            last = None
            for c in range(8):
                for lh, rh in [(afh, fm2_h), (afh, fm2_l), (afl, fm2_h)]:
                    last = t.matmul(psU[0:16, :], lh[:, 16 * c:16 * c + 16],
                                    rh[:, 512 * c:512 * c + 512],
                                    start=first, stop=False,
                                    skip_group_check=True)
                    first = False
            last.then_inc(sT.h, 1)
            rT_t = sT.n + 1
            for tt in range(4):
                gt = 4 * mt + tt
                t.wait_ge(sV.h, rT_v_arr[mt])
                if gt >= 2:
                    t.wait_ge(sV.h, ep_v[gt - 2])
                ypb = psO if tt % 2 == 0 else psB
                last = None
                for fo in range(2):
                    first = True
                    for c in range(8):
                        for lh, rh in [(fm2_h, wfh), (fm2_l, wfh), (fm2_h, wfl)]:
                            last = t.matmul(
                                ypb[:, 512 * fo:512 * fo + 512],
                                lh[:, 512 * c + 128 * tt:512 * c + 128 * tt + 128],
                                rh[:, 1024 * c + 512 * fo:1024 * c + 512 * fo + 512],
                                start=first, stop=False, skip_group_check=True)
                            first = False
                    for lh, rh in [(rTh, bmh), (rTl, bmh), (rTh, bml)]:
                        last = t.matmul(ypb[:, 512 * fo:512 * fo + 512],
                                        lh[0:16, 128 * tt:128 * tt + 128],
                                        rh[:, 512 * fo:512 * fo + 512],
                                        start=False, stop=False,
                                        skip_group_check=True)
                for fo in range(2):
                    if tt % 2 == 0:
                        sout = psA[:, 512 * fo:512 * fo + 512]
                    else:
                        sout = (psU if fo == 0 else psV)[:, 0:512]
                    first = True
                    for pgc in range(8):
                        last = t.matmul(
                            sout,
                            aor[:, 512 * pgc + 128 * tt:512 * pgc + 128 * tt + 128],
                            woTs[:, 1024 * pgc + 512 * fo:1024 * pgc + 512 * fo + 512],
                            start=first, stop=False, skip_group_check=True)
                        first = False
                last.then_inc(sT.h, 1)
        sT.n += 5
        p2t_end[mt] = sT.n
        for tt in range(4):
            ffn_t[4 * mt + tt] = sT.n - 3 + tt

        @block.vector
        def _(v, mt=mt):
            v.wait_ge(sT.h, sT.n - 4)
            i = v.tensor_copy(rfb[:], psU[0:16, :])
            fence(v, i)
            i = v.tensor_copy(rTh[:], rfb[:])
            fence(v, i)
            v.tensor_tensor(rTl[:], rfb[:], rTh[:],
                            op=OP.subtract).then_inc(sV.h, 1)
        sV.n += 1
        assert rT_v_arr[mt] == sV.n

        for tt in range(4):
            gt = 4 * mt + tt
            ypb = psO if tt % 2 == 0 else psB

            @block.vector
            def _(v, tt=tt, gt=gt, ypb=ypb):
                v.wait_ge(sT.h, ffn_t[gt])
                v.tensor_tensor(ybuf[:], ypb[:], fb_b[:],
                                op=OP.add).then_inc(sV.h, 1)
            sV.n += 1

            @block.scalar
            def _(s, gt=gt):
                s.wait_ge(sV.h, sV.n)
                s.activation(gbuf[:], ybuf[:], AF.Silu, bias=0.0,
                             scale=1.0).then_inc(sS.h, 1)
            sS.n += 1
            assert sil_s[gt] == sS.n

            @block.vector
            def _(v, tt=tt, gt=gt, mt=mt):
                v.wait_ge(sS.h, sil_s[gt])
                if tt % 2 == 0:
                    i = v.tensor_tensor(sbuf2[:], psA[:], bo_b[:], op=OP.add)
                else:
                    v.tensor_tensor(sbuf2[:, 0:512], psU[:], bo_b[:, 0:512],
                                    op=OP.add)
                    i = v.tensor_tensor(sbuf2[:, 512:1024], psV[:],
                                        bo_b[:, 512:1024], op=OP.add)
                fence(v, i)
                i = v.tensor_tensor(sbuf2[:], sbuf2[:], gbuf[:], op=OP.mult)
                fence(v, i)
                if gt >= 2:
                    v.wait_ge(dO.h, 16 * (gt - 1))
                v.tensor_tensor(outst[tt % 2][:], xres[tt][:], sbuf2[:],
                                op=OP.add).then_inc(sV.h, 1)
            sV.n += 1
            assert ep_v[gt] == sV.n

            @block.sync
            def _(sy, tt=tt, t0=t0, gt=gt):
                sy.wait_ge(sV.h, ep_v[gt])
                sy.dma_start(yd[t0 + 128 * tt:t0 + 128 * tt + 128, :],
                             outst[tt % 2][:]).then_inc(dO.h, 16)
            out_tiles += 1

    @block.sync
    def _(sy):
        sy.wait_ge(dO.h, 16 * out_tiles)

    nc.leave_named_scope("p2", _sid_p2, False)
    st_p2.close()
    st_all.close()
    return nc


_NC_CACHE = [None]


def kernel(L0_attn_x, L0_attn_Wq, L0_attn_bq, L0_attn_Wk, L0_attn_bk,
           L0_attn_Wv, L0_attn_bv, L0_attn_Wo, L0_attn_bo,
           L0_attn_prior_lambda, L0_ffn_kernel, L0_ffn_bias, L0_ffn_a,
           L0_ffn_b):
    x = np.ascontiguousarray(np.asarray(L0_attn_x, dtype=np.float32))
    wqT = np.ascontiguousarray(np.asarray(L0_attn_Wq, np.float32).T)
    wkT = np.ascontiguousarray(np.asarray(L0_attn_Wk, np.float32).T)
    wvT = np.ascontiguousarray(np.asarray(L0_attn_Wv, np.float32).T)
    woT = np.ascontiguousarray(np.asarray(L0_attn_Wo, np.float32).T)
    fk = np.ascontiguousarray(np.asarray(L0_ffn_kernel, np.float32))
    fa = np.ascontiguousarray(np.asarray(L0_ffn_a, np.float32))
    fbmat = np.ascontiguousarray(np.asarray(L0_ffn_b, np.float32))
    lam = np.asarray(L0_attn_prior_lambda, np.float32).reshape(1, 1)

    p = np.arange(128, dtype=np.int64)[:, None]
    j = np.arange(384, dtype=np.int64)[None, :]
    delta_real = (j - p - 128).astype(np.float32)
    delta_far = (j - p + 3000).astype(np.float32)
    e2m = np.zeros((2, 128), dtype=ml_dtypes.bfloat16)
    e2m[0, 0:64] = 1
    e2m[1, 64:128] = 1
    identf = np.eye(128, dtype=np.float32)
    identb = np.eye(128, dtype=ml_dtypes.bfloat16)

    if _NC_CACHE[0] is None:
        _NC_CACHE[0] = build()
    nc = _NC_CACHE[0]

    in_maps = []
    for c in range(8):
        b, half = c // 2, c % 2
        xb = np.ascontiguousarray(np.roll(x[b], -TMY * half, axis=0))
        in_maps.append({
            "x": xb, "wqT": wqT, "wkT": wkT, "wvT": wvT, "woT": woT,
            "bq": np.asarray(L0_attn_bq, np.float32),
            "bk": np.asarray(L0_attn_bk, np.float32),
            "bv": np.asarray(L0_attn_bv, np.float32),
            "bo": np.asarray(L0_attn_bo, np.float32),
            "lam": lam, "fk": fk,
            "fb": np.asarray(L0_ffn_bias, np.float32),
            "fa": fa, "fbm": fbmat,
            "dlc": delta_real,
            "dl0": delta_real if half == 0 else delta_far,
            "dl1": delta_real if half == 1 else delta_far,
            "e2": e2m, "identf": identf, "identb": identb,
        })
    res = run_bass_kernel_spmd(nc, in_maps, core_ids=list(range(8)))
    out = np.empty((4, L, D), dtype=np.float32)
    for c in range(8):
        b, half = c // 2, c % 2
        out[b, TMY * half:TMY * half + TMY, :] = res.results[c]["y"]
    return out


# version 15
# speedup vs baseline: 1.0865x; 1.0865x over previous
import numpy as np
import ml_dtypes
import concourse.bass as bass
import concourse.mybir as mybir
from concourse.bass_utils import run_bass_kernel_spmd
from concourse.alu_op_type import AluOpType as OP
from contextlib import ExitStack

F32 = mybir.dt.float32
BF16 = mybir.dt.bfloat16
AF = mybir.ActivationFunctionType
AX = mybir.AxisListType

D = 1024
L = 4096
TMY = 2048
NKC = 32
NQC = 4
QW = 512
NB = 4
C_SHIFT = 12.0
NTILE = 32
EPS = 1e-5


class Sem:
    __slots__ = ("h", "n")

    def __init__(self, h):
        self.h = h
        self.n = 0


def _band_pairs():
    pairs = []
    for qc in range(4):
        for k in range(max(0, 4 * qc - 1), min(15, 4 * qc + 4) + 1):
            pairs.append((qc, k, "c", 512 * qc - 128 * k + 128))
    pairs.append((3, 16, "0", 512 * 3 - 128 * 16 + 128))
    pairs.append((0, 31, "1", 256))
    out = {}
    for qc, k, tb, c0 in pairs:
        j_lo = max(0, -c0)
        j_hi = min(QW, 384 - c0)
        assert j_hi > j_lo, (qc, k, tb, c0)
        out.setdefault((qc, k), []).append((tb, c0, j_lo, j_hi))
    return out


BANDS = _band_pairs()


def build():
    nc = bass.Bass("TRN2", target_bir_lowering=False)
    xd = nc.dram_tensor("x", [L, D], F32, kind="ExternalInput").ap()
    wqT = nc.dram_tensor("wqT", [D, D], F32, kind="ExternalInput").ap()
    wkT = nc.dram_tensor("wkT", [D, D], F32, kind="ExternalInput").ap()
    wvT = nc.dram_tensor("wvT", [D, D], F32, kind="ExternalInput").ap()
    woT = nc.dram_tensor("woT", [D, D], F32, kind="ExternalInput").ap()
    bqd = nc.dram_tensor("bq", [D], F32, kind="ExternalInput").ap()
    bkd = nc.dram_tensor("bk", [D], F32, kind="ExternalInput").ap()
    bvd = nc.dram_tensor("bv", [D], F32, kind="ExternalInput").ap()
    bod = nc.dram_tensor("bo", [D], F32, kind="ExternalInput").ap()
    lamd = nc.dram_tensor("lam", [1, 1], F32, kind="ExternalInput").ap()
    fkd = nc.dram_tensor("fk", [D, D], F32, kind="ExternalInput").ap()
    fbd = nc.dram_tensor("fb", [D], F32, kind="ExternalInput").ap()
    fad = nc.dram_tensor("fa", [D, 16], F32, kind="ExternalInput").ap()
    fbm = nc.dram_tensor("fbm", [16, D], F32, kind="ExternalInput").ap()
    dlc = nc.dram_tensor("dlc", [128, 384], F32, kind="ExternalInput").ap()
    dl0 = nc.dram_tensor("dl0", [128, 384], F32, kind="ExternalInput").ap()
    dl1 = nc.dram_tensor("dl1", [128, 384], F32, kind="ExternalInput").ap()
    e2d = nc.dram_tensor("e2", [2, 128], BF16, kind="ExternalInput").ap()
    idd = nc.dram_tensor("identf", [128, 128], F32, kind="ExternalInput").ap()
    idbd = nc.dram_tensor("identb", [128, 128], BF16, kind="ExternalInput").ap()
    xnh_s = nc.dram_tensor("xnh_s", [8, 128, L], BF16).ap()
    xnl_s = nc.dram_tensor("xnl_s", [8, 128, L], BF16).ap()
    aos = nc.dram_tensor("aos", [8, 128, TMY], F32).ap()
    yd = nc.dram_tensor("y", [TMY, D], F32, kind="ExternalOutput").ap()

    st_all = ExitStack()
    e = st_all.enter_context
    dL = Sem(e(nc.semaphore("dL")))
    dG = Sem(e(nc.semaphore("dG")))
    dO = Sem(e(nc.semaphore("dO")))
    sT = Sem(e(nc.semaphore("sT")))
    sV = Sem(e(nc.semaphore("sV")))
    sS = Sem(e(nc.semaphore("sS")))
    sW = Sem(e(nc.semaphore("sW")))
    sWs = Sem(e(nc.semaphore("sWs")))
    sG = Sem(e(nc.semaphore("sG")))

    def fence(eng, instr, sem=None):
        s_ = sem if sem is not None else sW
        instr.then_inc(s_.h, 1)
        s_.n += 1
        eng.wait_ge(s_.h, s_.n)

    psS2 = e(nc.psum_tensor("psS2", [128, 2048], F32))
    psA = psS2[:, 0:1024]
    psB = psS2[:, 1024:2048]
    psO = e(nc.psum_tensor("psO", [128, 1024], F32))
    psU = e(nc.psum_tensor("psU", [128, 512], F32))
    psV = e(nc.psum_tensor("psV", [128, 512], F32))

    identf = e(nc.sbuf_tensor("sb_idf", [128, 128], F32))
    identb = e(nc.sbuf_tensor("sb_idb", [128, 128], BF16))
    e2 = e(nc.sbuf_tensor("sb_e2", [2, 128], BF16))
    onesb = e(nc.sbuf_tensor("onesb", [128, 1], BF16))
    onesf = e(nc.sbuf_tensor("onesf", [1, 128], F32))
    negc = e(nc.sbuf_tensor("negc", [128, 1], F32))
    tbl = {t: e(nc.sbuf_tensor(f"tbl{t}", [128, 384], F32)) for t in "c01"}
    lamc = e(nc.sbuf_tensor("lamc", [128, 1], F32))
    bqc = e(nc.sbuf_tensor("bqc", [128, 8], F32))
    bkc = e(nc.sbuf_tensor("bkc", [128, 8], F32))
    bvb = e(nc.sbuf_tensor("bvb", [128, D], F32))
    rowt = e(nc.sbuf_tensor("rowt", [1, D], F32))
    sevens = e(nc.sbuf_tensor("sevens", [1, 1], F32))
    r7 = e(nc.sbuf_tensor("r7", [1, 1], F32))

    block = e(nc.Block())
    ev_v_last = [0]

    @block.sync
    def _(sy):
        for dst, src in [(identf, idd), (identb, idbd), (e2, e2d),
                         (tbl["c"], dlc), (tbl["0"], dl0), (tbl["1"], dl1)]:
            sy.dma_start(dst[:], src).then_inc(dL.h, 16)
        sy.dma_start(bqc[:], bqd.rearrange("(c p) -> p c", p=128), allow_slow_non_contiguous=True).then_inc(dL.h, 16)
        sy.dma_start(bkc[:], bkd.rearrange("(c p) -> p c", p=128), allow_slow_non_contiguous=True).then_inc(dL.h, 16)
        sy.dma_start(rowt[:], bvd.rearrange("(o d) -> o d", o=1)).then_inc(dL.h, 16)
        sy.dma_start(lamc[0:1, 0:1], lamd).then_inc(dL.h, 16)
    dL.n += 10 * 16

    @block.gpsimd
    def _(g):
        g.memset(onesb[:], 1.0)
        g.memset(onesf[:], 1.0)
        g.memset(sevens[:], 7.0)
        g.memset(negc[:], -C_SHIFT).then_inc(sG.h, 1)
    sG.n += 1

    @block.tensor
    def _(t):
        t.wait_ge(dL.h, dL.n)
        t.wait_ge(sG.h, sG.n)
        t.matmul(psU[0:128, 0:1], onesf[:], lamc[0:1, 0:1], start=True,
                 stop=True, skip_group_check=True)
        t.matmul(psA[:, 0:512], onesf[:], rowt[0:1, 0:512], start=True,
                 stop=True, skip_group_check=True)
        t.matmul(psA[:, 512:1024], onesf[:], rowt[0:1, 512:1024], start=True,
                 stop=True, skip_group_check=True).then_inc(sT.h, 1)
    sT.n += 1

    @block.vector
    def _(v):
        v.wait_ge(sT.h, sT.n)
        v.tensor_copy(lamc[:], psU[0:128, 0:1])
        i = v.tensor_copy(bvb[:], psA[:])
        fence(v, i)
        v.reciprocal(r7[:], sevens[:]).then_inc(sV.h, 1)
    sV.n += 1

    msk1 = e(nc.sbuf_tensor("msk1", [128, 384], F32))
    msk2 = e(nc.sbuf_tensor("msk2", [128, 384], F32))
    for tname in "c01":
        tb = tbl[tname]

        @block.vector
        def _(v, tb=tb):
            v.wait_ge(sV.h, sV.n)
            i = v.scalar_tensor_tensor(msk1[:], tb[:], -1.0, tb[:],
                                       op0=OP.mult, op1=OP.max)
            fence(v, i)
            v.tensor_scalar_add(msk2[:], msk1[:], -30.0)
            v.tensor_scalar_add(msk1[:], msk1[:], -1.0).then_inc(sV.h, 1)
        sV.n += 1

        @block.scalar
        def _(s):
            s.wait_ge(sV.h, sV.n)
            i = s.activation(msk1[:], msk1[:], AF.Square, bias=0.0, scale=1.0)
            i = s.activation(msk2[:], msk2[:], AF.Square, bias=0.0, scale=1.0)
            fence(s, i, sWs)
            s.activation(msk1[:], msk1[:], AF.Exp, bias=0.0, scale=-2.0)
            s.activation(msk2[:], msk2[:], AF.Exp, bias=0.0,
                         scale=-0.125).then_inc(sS.h, 1)
        sS.n += 1

        @block.vector
        def _(v, tb=tb):
            v.wait_ge(sS.h, sS.n)
            i = v.tensor_scalar_mul(msk1[:], msk1[:], 0.8)
            fence(v, i)
            i = v.scalar_tensor_tensor(tb[:], msk2[:], 0.4, msk1[:],
                                       op0=OP.mult, op1=OP.add)
            fence(v, i)
            v.tensor_scalar_mul(tb[:], tb[:], lamc[:, 0:1]).then_inc(sV.h, 1)
        sV.n += 1

    _sid_ln = nc.enter_named_scope("lnphase", False)[0]
    st_p0 = ExitStack()
    e0 = st_p0.enter_context
    xstage = [e0(nc.sbuf_tensor(f"xstage{i}", [128, D], F32)) for i in range(8)]
    xm4 = e0(nc.sbuf_tensor("xm4", [128, 4 * D], F32))
    xnb4 = e0(nc.sbuf_tensor("xnb4", [128, 4 * D], F32))
    xnh4 = e0(nc.sbuf_tensor("xnh4", [128, 4 * D], BF16))
    xnl4 = e0(nc.sbuf_tensor("xnl4", [128, 4 * D], BF16))
    sqd = e0(nc.sbuf_tensor("sqd", [128, D], F32))
    fms_h = e0(nc.sbuf_tensor("fms_h", [128, 8 * 512], BF16))
    fms_l = e0(nc.sbuf_tensor("fms_l", [128, 8 * 512], BF16))
    st4 = [e0(nc.sbuf_tensor(f"st4_{i}", [128, 4], F32)) for i in range(6)]

    ev_v = [0] * 8
    scw_g = [0] * 8
    ln_v = [0] * 8
    trp_t = [0] * 8
    ln_s = [0] * 8
    for g in range(8):
        @block.sync
        def _(sy, g=g):
            for i in range(4):
                ti = 4 * g + i
                if ti >= 8:
                    sy.wait_ge(sV.h, ln_v[(ti - 8) // 4])
                sy.dma_start(xstage[ti % 8][:],
                             xd[128 * ti:128 * ti + 128, :]).then_inc(dL.h, 16)
        dL.n += 4 * 16

        @block.vector
        def _(v, g=g):
            v.wait_ge(dL.h, dL.n)
            if g >= 1:
                v.wait_ge(sS.h, ln_s[g - 1])
            for i_ in range(4):
                i = v.reduce_sum(st4[0][:, i_:i_ + 1], xstage[(4 * g + i_) % 8][:],
                                 axis=AX.X)
                fence(v, i)
            i = v.tensor_scalar_mul(st4[1][:], st4[0][:], -1.0 / 1024.0)
            fence(v, i)
            for i_ in range(4):
                i = v.tensor_scalar_add(xm4[:, 1024 * i_:1024 * i_ + 1024],
                                        xstage[(4 * g + i_) % 8][:],
                                        st4[1][:, i_:i_ + 1])
            i.then_inc(sV.h, 1)
        sV.n += 1

        @block.scalar
        def _(s, g=g):
            s.wait_ge(sV.h, sV.n)
            i = None
            for i_ in range(4):
                i = s.activation(sqd[:], xm4[:, 1024 * i_:1024 * i_ + 1024],
                                 AF.Square, bias=0.0, scale=1.0,
                                 accum_out=st4[2][:, i_:i_ + 1])
            i.then_inc(sS.h, 1)
        sS.n += 1

        @block.vector
        def _(v):
            v.wait_ge(sS.h, sS.n)
            v.tensor_scalar(st4[3][:], st4[2][:], 1.0 / 1024.0, EPS,
                            op0=OP.mult, op1=OP.add).then_inc(sV.h, 1)
        sV.n += 1

        @block.scalar
        def _(s):
            s.wait_ge(sV.h, sV.n)
            s.activation(st4[4][:], st4[3][:], AF.Sqrt, bias=0.0,
                         scale=1.0).then_inc(sS.h, 1)
        sS.n += 1

        @block.vector
        def _(v, g=g):
            v.wait_ge(sS.h, sS.n)
            if g >= 1:
                v.wait_ge(sT.h, trp_t[g - 1])
            i = v.reciprocal(st4[5][:], st4[4][:])
            fence(v, i)
            i = v.tensor_mul(st4[0][:], st4[5][:], st4[5][:])
            fence(v, i)
            i = v.tensor_mul(st4[1][:], st4[0][:], st4[3][:])
            fence(v, i)
            i = v.tensor_scalar(st4[0][:], st4[1][:], -0.5, 1.5,
                                op0=OP.mult, op1=OP.add)
            fence(v, i)
            i = v.tensor_mul(st4[1][:], st4[5][:], st4[0][:])
            fence(v, i)
            for i_ in range(4):
                i = v.tensor_scalar_mul(xnb4[:, 1024 * i_:1024 * i_ + 1024],
                                        xm4[:, 1024 * i_:1024 * i_ + 1024],
                                        st4[1][:, i_:i_ + 1])
            fence(v, i)
            i = v.tensor_copy(xnh4[:], xnb4[:])
            fence(v, i)
            v.tensor_tensor(xnl4[:], xnb4[:], xnh4[:],
                            op=OP.subtract).then_inc(sV.h, 1)
        sV.n += 1
        ln_v[g] = sV.n
        ln_s[g] = sS.n

        for i_ in range(4):
            @block.tensor
            def _(t, g=g, i_=i_):
                t.wait_ge(sV.h, ln_v[g])
                if not (g == 0 and i_ == 0):
                    t.wait_ge(sV.h, ev_v_last[0])
                last = None
                for c in range(8):
                    last = t.matmul(psA[:, 64 * c:64 * c + 64].bitcast(BF16),
                                    xnh4[:, 1024 * i_ + 128 * c:
                                         1024 * i_ + 128 * c + 128],
                                    identb[:], is_transpose=True, start=True,
                                    stop=True, skip_group_check=True)
                    last.then_inc(sT.h, 1)
                    last = t.matmul(psB[:, 64 * c:64 * c + 64].bitcast(BF16),
                                    xnl4[:, 1024 * i_ + 128 * c:
                                         1024 * i_ + 128 * c + 128],
                                    identb[:], is_transpose=True, start=True,
                                    stop=True, skip_group_check=True)
                    last.then_inc(sT.h, 1)
            sT.n += 16

            @block.vector
            def _(v, g=g, i_=i_):
                v.wait_ge(sT.h, sT.n)
                if g >= 1 and i_ == 0:
                    v.wait_ge(dG.h, scw_g[g - 1])
                fmt_h = fms_h.tensor if hasattr(fms_h, "tensor") else fms_h
                fmt_l = fms_l.tensor if hasattr(fms_l, "tensor") else fms_l
                dsth = bass.AP(fmt_h, 128 * i_, [[4096, 128], [512, 8], [1, 128]])
                dstl = bass.AP(fmt_l, 128 * i_, [[4096, 128], [512, 8], [1, 128]])
                v.tensor_copy(dsth, psA[:, 0:512].bitcast(BF16))
                i = v.tensor_copy(dstl, psB[:, 0:512].bitcast(BF16))
                i.then_inc(sV.h, 1)
            sV.n += 1
            ev_v_last[0] = sV.n
        ev_v[g] = sV.n
        trp_t[g] = sT.n

        @block.gpsimd
        def _(g_, g=g):
            g_.wait_ge(sV.h, ev_v[g])
            for c in range(8):
                g_.dma_start(xnh_s[c, :, 512 * g:512 * g + 512],
                             fms_h[:, 512 * c:512 * c + 512]).then_inc(dG.h, 16)
                g_.dma_start(xnl_s[c, :, 512 * g:512 * g + 512],
                             fms_l[:, 512 * c:512 * c + 512]).then_inc(dG.h, 16)
        dG.n += 16 * 16
        scw_g[g] = dG.n

    st_p0.close()
    st_p1 = ExitStack()
    e1 = st_p1.enter_context
    khi = e1(nc.sbuf_tensor("khi", [128, 2 * L], BF16))
    klo = e1(nc.sbuf_tensor("klo", [128, 2 * L], BF16))
    vhi = e1(nc.sbuf_tensor("vhi", [128, 32 * 256], BF16))
    vlo = e1(nc.sbuf_tensor("vlo", [128, 32 * 256], BF16))
    qhi = e1(nc.sbuf_tensor("qhi", [128, 2 * TMY], BF16))
    qlo = e1(nc.sbuf_tensor("qlo", [128, 2 * TMY], BF16))
    wst = None
    wsp = {}
    for wn in ("q", "k", "v"):
        wsp[wn] = (e1(nc.sbuf_tensor(f"w{wn}h", [128, 2048], BF16)),
                   e1(nc.sbuf_tensor(f"w{wn}l", [128, 2048], BF16)))
    fmr_h = e1(nc.sbuf_tensor("fmr_h", [128, 8 * 512], BF16))
    fmr_l = e1(nc.sbuf_tensor("fmr_l", [128, 8 * 512], BF16))
    wst = fmr_h[:, 0:4096].bitcast(F32)
    evr = e1(nc.sbuf_tensor("evr", [128, 1024], F32))
    bvb4 = e1(nc.sbuf_tensor("bvb4", [128, 1024], F32))
    pf2 = [e1(nc.sbuf_tensor(f"pf2{i}", [128, 2048], F32)) for i in range(2)]
    ph2 = [e1(nc.sbuf_tensor(f"ph2{i}", [128, 2048], BF16)) for i in range(2)]
    pl2 = [e1(nc.sbuf_tensor(f"pl2{i}", [128, 2048], BF16)) for i in range(2)]
    aost = [e1(nc.sbuf_tensor(f"aost{i}", [128, 1024], F32)) for i in range(2)]
    z_sb = e1(nc.sbuf_tensor("z_sb", [128, 1024], F32))
    rzb = e1(nc.sbuf_tensor("rzb", [128, 16], F32))
    zsum8 = e1(nc.sbuf_tensor("zsum8", [128, 16], F32))
    zt_sb = e1(nc.sbuf_tensor("zt_sb", [128, 1024], F32))
    rzT = e1(nc.sbuf_tensor("rzT", [2, 1024], F32))
    rzTh = e1(nc.sbuf_tensor("rzTh", [2, 1024], BF16))
    rzTl = e1(nc.sbuf_tensor("rzTl", [2, 1024], BF16))
    b_sb = e1(nc.sbuf_tensor("b_sb", [128, 1024], F32))

    ao_par = 0
    ao_dg = [0, 0]
    norm_end_v = 0
    prj_t_last = [0]
    for hb in range(NB):
        _sid_p = nc.enter_named_scope(f"proj{hb}", False)[0]
        for wi, (wn, wt_d) in enumerate([("q", wqT), ("k", wkT), ("v", wvT)]):
            @block.sync
            def _(sy, wn=wn, wt_d=wt_d, hb=hb, pt=prj_t_last[0], wi=wi,
                  psv=sV.n):
                sy.wait_ge(sV.h, psv)
                if wi == 0 and hb == 0:
                    sy.wait_ge(dG.h, scw_g[7])
                src = bass.AP(wt_d.tensor, 256 * hb,
                              [[1024, 128], [131072, 8], [1, 256]])
                sy.dma_start(wst[:], src).then_inc(dL.h, 16)
            dL.n += 16

            @block.vector
            def _(v, wn=wn):
                v.wait_ge(dL.h, dL.n)
                wh, wl = wsp[wn]
                i = v.tensor_copy(wh[:], wst[:])
                fence(v, i)
                v.tensor_tensor(wl[:], wst[:], wh[:],
                                op=OP.subtract).then_inc(sV.h, 1)
            sV.n += 1

        @block.vector
        def _(v, hb=hb):
            v.wait_ge(sV.h, sV.n)
            for r in range(4):
                i = v.tensor_copy(bvb4[:, 256 * r:256 * r + 256],
                                  bvb[:, 256 * hb:256 * hb + 256])
            i.then_inc(sV.h, 1)
        sV.n += 1
        w_ready_v = sV.n

        prj_t = 0
        prj_v = 0
        for mt in range(8):
            t0 = 512 * mt

            @block.sync
            def _(sy, mt=mt, t0=t0, prj_t=prj_t, hb=hb, wrv=w_ready_v):
                if hb == 0:
                    sy.wait_ge(dG.h, scw_g[7])
                if mt == 0:
                    sy.wait_ge(sV.h, wrv)
                if prj_t:
                    sy.wait_ge(sT.h, prj_t)
                for c in range(8):
                    sy.dma_start(fmr_h[:, 512 * c:512 * c + 512],
                                 xnh_s[c, :, t0:t0 + 512]).then_inc(dL.h, 16)
                    sy.dma_start(fmr_l[:, 512 * c:512 * c + 512],
                                 xnl_s[c, :, t0:t0 + 512]).then_inc(dL.h, 16)
            dL.n += 16 * 16

            @block.tensor
            def _(t, mt=mt, prj_v=prj_v, w_ready_v=w_ready_v,
                  nev=norm_end_v, ss=sS.n):
                t.wait_ge(dL.h, dL.n)
                t.wait_ge(sV.h, max(w_ready_v, prj_v, nev))
                if mt == 0:
                    t.wait_ge(sS.h, ss)
                wqh, wql = wsp["q"]
                wkh, wkl = wsp["k"]
                wvh, wvl = wsp["v"]
                last = None
                for kc in range(2):
                    first = True
                    for c in range(8):
                        for lh, rh in [(wkh, fmr_h), (wkh, fmr_l), (wkl, fmr_h)]:
                            last = t.matmul(
                                psA[:, 512 * kc:512 * kc + 512],
                                lh[:, 256 * c + 128 * kc:256 * c + 128 * kc + 128],
                                rh[:, 512 * c:512 * c + 512],
                                start=first, stop=False, skip_group_check=True)
                            first = False
                for s4 in range(4):
                    first = True
                    for c in range(8):
                        for lh, rh in [(fmr_h, wvh), (fmr_l, wvh), (fmr_h, wvl)]:
                            last = t.matmul(
                                psB[:, 256 * s4:256 * s4 + 256],
                                lh[:, 512 * c + 128 * s4:512 * c + 128 * s4 + 128],
                                rh[:, 256 * c:256 * c + 256],
                                start=first, stop=False, skip_group_check=True)
                            first = False
                if mt < 4:
                    for kc in range(2):
                        first = True
                        for c in range(8):
                            for lh, rh in [(wqh, fmr_h), (wqh, fmr_l), (wql, fmr_h)]:
                                last = t.matmul(
                                    psO[:, 512 * kc:512 * kc + 512],
                                    lh[:, 256 * c + 128 * kc:256 * c + 128 * kc + 128],
                                    rh[:, 512 * c:512 * c + 512],
                                    start=first, stop=False,
                                    skip_group_check=True)
                                first = False
                last.then_inc(sT.h, 1)
            sT.n += 1
            prj_t = sT.n

            @block.vector
            def _(v, mt=mt, t0=t0, hb=hb):
                v.wait_ge(sT.h, sT.n)
                v.tensor_scalar_add(evr[:, 0:512], psA[:, 0:512],
                                    bkc[:, 2 * hb:2 * hb + 1])
                i = v.tensor_scalar_add(evr[:, 512:1024], psA[:, 512:1024],
                                        bkc[:, 2 * hb + 1:2 * hb + 2])
                fence(v, i)
                for kc in range(2):
                    i = v.tensor_copy(khi[:, L * kc + t0:L * kc + t0 + 512],
                                      evr[:, 512 * kc:512 * kc + 512])
                fence(v, i)
                for kc in range(2):
                    i = v.tensor_tensor(klo[:, L * kc + t0:L * kc + t0 + 512],
                                        evr[:, 512 * kc:512 * kc + 512],
                                        khi[:, L * kc + t0:L * kc + t0 + 512],
                                        op=OP.subtract)
                fence(v, i)
                i = v.tensor_tensor(evr[:], psB[:], bvb4[:], op=OP.add)
                fence(v, i)
                i = v.tensor_copy(vhi[:, 1024 * mt:1024 * mt + 1024], evr[:])
                fence(v, i)
                i = v.tensor_tensor(vlo[:, 1024 * mt:1024 * mt + 1024], evr[:],
                                    vhi[:, 1024 * mt:1024 * mt + 1024],
                                    op=OP.subtract)
                if mt < 4:
                    fence(v, i)
                    v.tensor_scalar(evr[:, 0:512], psO[:, 0:512],
                                    bqc[:, 2 * hb:2 * hb + 1], 0.125,
                                    op0=OP.add, op1=OP.mult)
                    i = v.tensor_scalar(evr[:, 512:1024], psO[:, 512:1024],
                                        bqc[:, 2 * hb + 1:2 * hb + 2], 0.125,
                                        op0=OP.add, op1=OP.mult)
                    fence(v, i)
                    for kc in range(2):
                        i = v.tensor_copy(qhi[:, TMY * kc + t0:TMY * kc + t0 + 512],
                                          evr[:, 512 * kc:512 * kc + 512])
                    fence(v, i)
                    for kc in range(2):
                        i = v.tensor_tensor(qlo[:, TMY * kc + t0:TMY * kc + t0 + 512],
                                            evr[:, 512 * kc:512 * kc + 512],
                                            qhi[:, TMY * kc + t0:TMY * kc + t0 + 512],
                                            op=OP.subtract)
                i.then_inc(sV.h, 1)
            sV.n += 1
            prj_v = sV.n

        nc.leave_named_scope(f"proj{hb}", _sid_p, False)
        _sid_a = nc.enter_named_scope(f"attn{hb}", False)[0]
        for qc in range(NQC):
            pg0, pg1 = 2 * hb, 2 * hb + 1
            qb = [TMY * 0 + QW * qc, TMY * 1 + QW * qc]
            entry_s = sS.n
            entry_v = max(sV.n, norm_end_v)
            entry_v0 = sV.n
            entry_t = sT.n
            nb_le = [0] * NKC
            acc = 0
            for k in range(NKC):
                if BANDS.get((qc, k)):
                    acc += 1
                nb_le[k] = acc
            qk_t = [entry_t + (1 if k == 0 else 2 * k) for k in range(NKC)]
            avz_t = [entry_t + (2 * k + 3 if k < NKC - 1 else 2 * NKC)
                     for k in range(NKC)]
            exp_s = [entry_s + 2 * k + 1 for k in range(NKC)]
            copy_s = [entry_s + 2 * k + 2 for k in range(NKC)]
            mask_v = [entry_v0 + nb_le[k] + k if BANDS.get((qc, k)) else 0
                      for k in range(NKC)]
            split_v = [entry_v0 + nb_le[k] + k + 1 for k in range(NKC)]
            V0 = split_v[NKC - 1]
            T0 = avz_t[NKC - 1]

            @block.tensor
            def _(t, qc=qc, qb=qb, prj_v=prj_v, entry_s=entry_s,
                  entry_v=entry_v, exp_s=exp_s, split_v=split_v, V0=V0,
                  copy_s=copy_s):
                def qkg(k):
                    if k >= 1:
                        t.wait_ge(sS.h, exp_s[k - 1])
                    else:
                        t.wait_ge(sS.h, entry_s)
                        t.wait_ge(sV.h, prj_v)
                    last = None
                    for pr in range(2):
                        for kb, qx in [(khi, qhi), (khi, qlo), (klo, qhi)]:
                            for h in range(2):
                                ko = L * pr + 128 * k
                                last = t.matmul(
                                    psS2[:, 1024 * pr + 512 * h:
                                         1024 * pr + 512 * h + 512],
                                    kb[64 * h:64 * h + 64, ko:ko + 128],
                                    qx[64 * h:64 * h + 64, qb[pr]:qb[pr] + QW],
                                    start=(kb is khi and qx is qhi),
                                    stop=(kb is klo),
                                    tile_position=(64 * h, 0),
                                    skip_group_check=True)
                    last.then_inc(sT.h, 1)

                def avzg(k):
                    t.wait_ge(sV.h, split_v[k])
                    t.wait_ge(sS.h, copy_s[k])
                    if k == 0:
                        t.wait_ge(sV.h, entry_v)
                    pi_ = k % 2
                    last = None
                    for pr, zbank in [(0, psU), (1, psV)]:
                        po = 1024 * pr
                        zops = [(ph2[pi_], 0, 0), (pl2[pi_], 32, 0),
                                (ph2[pi_], 64, 1), (pl2[pi_], 96, 1)]
                        for pb, col, h in zops:
                            t.matmul(zbank[col:col + 1, :], onesb[:],
                                     pb[:, po + 512 * h:po + 512 * h + 512],
                                     start=(k == 0), stop=False,
                                     tile_position=(0, col),
                                     skip_group_check=True)
                        for vb, pb in [(vhi, ph2[pi_]), (vlo, ph2[pi_]),
                                       (vhi, pl2[pi_])]:
                            for h in range(2):
                                last = t.matmul(
                                    psO[64 * h:64 * h + 64,
                                        512 * pr:512 * pr + 512],
                                    vb[:, 256 * k + 128 * pr + 64 * h:
                                       256 * k + 128 * pr + 64 * h + 64],
                                    pb[:, po + 512 * h:po + 512 * h + 512],
                                    start=(k == 0 and vb is vhi and pb is ph2[pi_]),
                                    stop=False,
                                    tile_position=(0, 64 * h),
                                    skip_group_check=True)
                    last.then_inc(sT.h, 1)

                qkg(0)
                for k in range(1, NKC):
                    qkg(k)
                    avzg(k - 1)
                avzg(NKC - 1)
                t.wait_ge(sV.h, V0 + 1)
                for j in range(4):
                    t.matmul(psU[:, 128 * j:128 * j + 128],
                             z_sb[:, 128 * j:128 * j + 128], identf[:],
                             is_transpose=True, start=True, stop=True,
                             skip_group_check=True).then_inc(sT.h, 1)
                    t.matmul(psV[:, 128 * j:128 * j + 128],
                             z_sb[:, 512 + 128 * j:512 + 128 * j + 128],
                             identf[:],
                             is_transpose=True, start=True, stop=True,
                             skip_group_check=True).then_inc(sT.h, 1)
                t.wait_ge(sV.h, V0 + 2)
                for J in range(8):
                    bank = psU if J < 4 else psV
                    t.matmul(bank[0:2, 128 * (J % 4):128 * (J % 4) + 128],
                             rzb[:, 2 * J:2 * J + 2], identf[:],
                             is_transpose=True, start=True, stop=True,
                             skip_group_check=True).then_inc(sT.h, 1)
                t.wait_ge(sV.h, V0 + 3)
                for pr, bank in [(0, psU), (1, psV)]:
                    t.matmul(bank[:], e2[:], rzTh[:, 512 * pr:512 * pr + 512],
                             start=True, stop=False, skip_group_check=True)
                    t.matmul(bank[:], e2[:], rzTl[:, 512 * pr:512 * pr + 512],
                             start=False, stop=True,
                             skip_group_check=True).then_inc(sT.h, 1)
            sT.n = T0 + 16 + 2
            T1 = sT.n

            @block.vector
            def _(v, qc=qc, exp_s=exp_s, avz_t=avz_t, qk_t=qk_t,
                  T0=T0, ao_par=ao_par):
                for k in range(NKC):
                    pi = k % 2
                    bands = BANDS.get((qc, k), [])
                    if bands:
                        v.wait_ge(sT.h, qk_t[k])
                        i = None
                        for tn, c0, j_lo, j_hi in bands:
                            for pr in range(2):
                                for h in range(2):
                                    o = 1024 * pr + 512 * h
                                    i = v.tensor_tensor(
                                        psS2[:, o + j_lo:o + j_hi],
                                        psS2[:, o + j_lo:o + j_hi],
                                        tbl[tn][:, j_lo + c0:j_hi + c0],
                                        op=OP.add)
                        i.then_inc(sV.h, 1)
                    v.wait_ge(sS.h, exp_s[k])
                    if k >= 2:
                        v.wait_ge(sT.h, avz_t[k - 2])
                    i = v.tensor_copy(ph2[pi][:, 1024:2048],
                                      pf2[pi][:, 1024:2048])
                    fence(v, i)
                    v.wait_ge(sS.h, copy_s[k])
                    v.tensor_tensor(pl2[pi][:], pf2[pi][:], ph2[pi][:],
                                    op=OP.subtract).then_inc(sV.h, 1)
                v.wait_ge(sT.h, avz_t[NKC - 1])
                v.tensor_copy(z_sb[:, 0:512], psU[:])
                v.tensor_copy(z_sb[:, 512:1024], psV[:]).then_inc(sV.h, 1)
                v.wait_ge(sT.h, T0 + 8)
                v.tensor_copy(zt_sb[:, 0:512], psU[:])
                i = v.tensor_copy(zt_sb[:, 512:1024], psV[:])
                fence(v, i)
                ztt = zt_sb.tensor if hasattr(zt_sb, "tensor") else zt_sb
                va = bass.AP(ztt, 0, [[1024, 128], [128, 8], [64, 2]])
                vb2 = bass.AP(ztt, 32, [[1024, 128], [128, 8], [64, 2]])
                i = v.tensor_tensor(zsum8[:, 0:16], va, vb2, op=OP.add)
                fence(v, i)
                v.reciprocal(rzb[:, 0:16], zsum8[:, 0:16]).then_inc(sV.h, 1)
                v.wait_ge(sT.h, T0 + 16)
                v.tensor_copy(rzT[:, 0:512], psU[0:2, :])
                i = v.tensor_copy(rzT[:, 512:1024], psV[0:2, :])
                fence(v, i)
                i = v.tensor_copy(rzTh[:], rzT[:])
                fence(v, i)
                v.tensor_tensor(rzTl[:], rzT[:], rzTh[:],
                                op=OP.subtract).then_inc(sV.h, 1)
                v.wait_ge(sT.h, T0 + 18)
                v.wait_ge(dG.h, ao_dg[ao_par])
                v.tensor_copy(b_sb[:, 0:512], psU[:])
                i = v.tensor_copy(b_sb[:, 512:1024], psV[:])
                fence(v, i)
                v.tensor_tensor(aost[ao_par][:, 0:512],
                                psO[:, 0:512], b_sb[:, 0:512], op=OP.mult)
                v.tensor_tensor(aost[ao_par][:, 512:1024],
                                psO[:, 512:1024], b_sb[:, 512:1024],
                                op=OP.mult).then_inc(sV.h, 1)
            sV.n = V0 + 4
            norm_end_v = sV.n

            @block.scalar
            def _(s, qc=qc, qk_t=qk_t, mask_v=mask_v, split_v=split_v):
                for k in range(NKC):
                    pi = k % 2
                    s.wait_ge(sT.h, qk_t[k])
                    if mask_v[k]:
                        s.wait_ge(sV.h, mask_v[k])
                    if k >= 2:
                        s.wait_ge(sV.h, split_v[k - 2])
                        s.wait_ge(sT.h, avz_t[k - 2])
                    i = s.activation(pf2[pi][:], psS2[:], AF.Exp,
                                     bias=negc[:], scale=1.0)
                    i.then_inc(sS.h, 1)
                    s.wait_ge(sS.h, exp_s[k])
                    s.copy(ph2[pi][:, 0:1024],
                           pf2[pi][:, 0:1024]).then_inc(sS.h, 1)
            sS.n += 2 * NKC

            @block.gpsimd
            def _(g, ao_par=ao_par, pg0=pg0, pg1=pg1, qc=qc):
                g.wait_ge(sV.h, sV.n)
                g.dma_start(aos[pg0, :, QW * qc:QW * qc + QW],
                            aost[ao_par][:, 0:512]).then_inc(dG.h, 16)
                g.dma_start(aos[pg1, :, QW * qc:QW * qc + QW],
                            aost[ao_par][:, 512:1024]).then_inc(dG.h, 16)
            dG.n += 32
            ao_dg[ao_par] = dG.n
            ao_par ^= 1

        nc.leave_named_scope(f"attn{hb}", _sid_a, False)
    st_p1.close()
    _sid_p2 = nc.enter_named_scope("p2", False)[0]
    st_p2 = ExitStack()
    e2s = st_p2.enter_context
    woTs = e2s(nc.sbuf_tensor("woTs", [128, 8 * 1024], F32))
    wfst = e2s(nc.sbuf_tensor("wfst", [128, 8 * 1024], F32))
    wfh = e2s(nc.sbuf_tensor("wfh", [128, 8 * 1024], BF16))
    wfl = e2s(nc.sbuf_tensor("wfl", [128, 8 * 1024], BF16))
    aor = e2s(nc.sbuf_tensor("aor", [128, 8 * 512], F32))
    fm2_h = e2s(nc.sbuf_tensor("fm2_h", [128, 8 * 512], BF16))
    fm2_l = e2s(nc.sbuf_tensor("fm2_l", [128, 8 * 512], BF16))
    xres = [e2s(nc.sbuf_tensor(f"xres{i}", [128, D], F32)) for i in range(4)]
    outst = [e2s(nc.sbuf_tensor(f"outst{i}", [128, D], F32)) for i in range(2)]
    ybuf = e2s(nc.sbuf_tensor("ybuf", [128, D], F32))
    gbuf = e2s(nc.sbuf_tensor("gbuf", [128, D], F32))
    sbuf2 = e2s(nc.sbuf_tensor("sbuf2", [128, D], F32))
    bo_b = e2s(nc.sbuf_tensor("bo_b", [128, D], F32))
    fb_b = e2s(nc.sbuf_tensor("fb_b", [128, D], F32))
    fm2B_h = wfst[:, 0:2048].bitcast(BF16)
    fm2B_l = wfst[:, 2048:4096].bitcast(BF16)
    aorB = wfst[:, 4096:8192]
    qt1 = aor[:, 0:1024]
    qt2 = aor[:, 1024:2048]
    qt3 = aor[:, 2048:3072]
    mk8 = e2s(nc.sbuf_tensor("mk8", [128, 8], F32))
    mcol = e2s(nc.sbuf_tensor("mcol", [128, 1], F32))
    mrow = e2s(nc.sbuf_tensor("mrow", [1, 128], F32))
    mm1 = e2s(nc.sbuf_tensor("mm1", [1, 1], F32))
    scA = e2s(nc.sbuf_tensor("scA", [1, 1], F32))
    invA = e2s(nc.sbuf_tensor("invA", [1, 1], F32))
    sic = e2s(nc.sbuf_tensor("sic", [128, 2], F32))
    afh = e2s(nc.sbuf_tensor("afh", [128, 128], BF16))
    afl = e2s(nc.sbuf_tensor("afl", [128, 128], BF16))
    afst = e2s(nc.sbuf_tensor("afst", [128, 128], F32))
    bmst = e2s(nc.sbuf_tensor("bmst", [16, D], F32))
    bmh = e2s(nc.sbuf_tensor("bmh", [16, D], BF16))
    bml = e2s(nc.sbuf_tensor("bml", [16, D], BF16))
    rfb = e2s(nc.sbuf_tensor("rfb", [16, 512], F32))
    rTh = e2s(nc.sbuf_tensor("rTh", [16, 512], BF16))
    rTl = e2s(nc.sbuf_tensor("rTl", [16, 512], BF16))

    @block.sync
    def _(sy):
        sy.wait_ge(sV.h, sV.n)
        sy.dma_start(woTs[:], bass.AP(woT.tensor, 0,
                     [[1024, 128], [131072, 8], [1, 1024]])).then_inc(dL.h, 16)
        sy.dma_start(wfst[:], bass.AP(fkd.tensor, 0,
                     [[1024, 128], [131072, 8], [1, 1024]])).then_inc(dL.h, 16)
        sy.dma_start(afst[:], bass.AP(fad.tensor, 0,
                     [[16, 128], [2048, 8], [1, 16]])).then_inc(dL.h, 16)
        sy.dma_start(bmst[:], fbm).then_inc(dL.h, 16)
        sy.dma_start(rowt[:], bod.rearrange("(o d) -> o d", o=1)).then_inc(dL.h, 16)
    dL.n += 5 * 16

    @block.tensor
    def _(t):
        t.wait_ge(dL.h, dL.n)
        t.matmul(psB[:, 0:512], onesf[:], rowt[0:1, 0:512], start=True,
                 stop=True, skip_group_check=True)
        t.matmul(psB[:, 512:1024], onesf[:], rowt[0:1, 512:1024], start=True,
                 stop=True, skip_group_check=True).then_inc(sT.h, 1)
    sT.n += 1

    @block.vector
    def _(v):
        v.wait_ge(sT.h, sT.n)
        v.tensor_copy(bo_b[:], psB[:]).then_inc(sV.h, 1)
    sV.n += 1

    @block.sync
    def _(sy):
        sy.wait_ge(sV.h, sV.n)
        sy.dma_start(rowt[:], fbd.rearrange("(o d) -> o d", o=1)).then_inc(dL.h, 16)
    dL.n += 16

    @block.tensor
    def _(t):
        t.wait_ge(dL.h, dL.n)
        t.matmul(psB[:, 0:512], onesf[:], rowt[0:1, 0:512], start=True,
                 stop=True, skip_group_check=True)
        t.matmul(psB[:, 512:1024], onesf[:], rowt[0:1, 512:1024], start=True,
                 stop=True, skip_group_check=True).then_inc(sT.h, 1)
    sT.n += 1

    @block.vector
    def _(v):
        v.wait_ge(sT.h, sT.n)
        v.tensor_copy(fb_b[:], psB[:]).then_inc(sV.h, 1)
    sV.n += 1

    @block.vector
    def _(v):
        v.wait_ge(dL.h, dL.n)
        for c in range(8):
            i = v.scalar_tensor_tensor(qt1[:], wfst[:, 1024 * c:1024 * c + 1024],
                                       -1.0, wfst[:, 1024 * c:1024 * c + 1024],
                                       op0=OP.mult, op1=OP.max)
            fence(v, i)
            i = v.reduce_max(mk8[:, c:c + 1], qt1[:], axis=AX.X)
            fence(v, i)
        i = v.reduce_max(mcol[:], mk8[:], axis=AX.X)
        i.then_inc(sV.h, 1)
    sV.n += 1

    @block.tensor
    def _(t):
        t.wait_ge(sV.h, sV.n)
        t.matmul(psU[0:1, 0:128], mcol[:], identf[:], is_transpose=True,
                 start=True, stop=True, skip_group_check=True).then_inc(sT.h, 1)
    sT.n += 1

    @block.vector
    def _(v):
        v.wait_ge(sT.h, sT.n)
        i = v.tensor_copy(mrow[:], psU[0:1, 0:128])
        fence(v, i)
        i = v.reduce_max(mm1[:], mrow[:], axis=AX.X)
        fence(v, i)
        i = v.tensor_scalar_mul(scA[:], mm1[:], r7[0:1, 0:1])
        fence(v, i)
        v.reciprocal(invA[:], scA[:]).then_inc(sV.h, 1)
    sV.n += 1

    @block.tensor
    def _(t):
        t.wait_ge(sV.h, sV.n)
        t.matmul(psU[0:128, 0:1], onesf[:], scA[:], start=True, stop=True,
                 skip_group_check=True)
        t.matmul(psU[0:128, 1:2], onesf[:], invA[:], start=True, stop=True,
                 skip_group_check=True).then_inc(sT.h, 1)
    sT.n += 1

    @block.vector
    def _(v):
        v.wait_ge(sT.h, sT.n)
        i = v.tensor_copy(sic[:], psU[0:128, 0:2])
        fence(v, i)
        for c in range(8):
            w_sl = wfst[:, 1024 * c:1024 * c + 1024]
            i = v.tensor_scalar(qt1[:], w_sl, sic[:, 1:2], 0.5,
                                op0=OP.mult, op1=OP.add)
            fence(v, i)
            i = v.tensor_scalar(qt2[:], qt1[:], -6.0, None, op0=OP.is_ge)
            fence(v, i)
            cur, nxt = qt2, qt3
            for kk in range(-5, 8):
                i = v.scalar_tensor_tensor(nxt[:], qt1[:], float(kk), cur[:],
                                           op0=OP.is_ge, op1=OP.add)
                fence(v, i)
                cur, nxt = nxt, cur
            i = v.tensor_scalar(cur[:], cur[:], 14.0, -7.0,
                                op0=OP.min, op1=OP.add)
            fence(v, i)
            i = v.tensor_scalar_mul(qt1[:], cur[:], sic[:, 0:1])
            fence(v, i)
            i = v.tensor_copy(wfh[:, 1024 * c:1024 * c + 1024], qt1[:])
            fence(v, i)
            i = v.tensor_tensor(wfl[:, 1024 * c:1024 * c + 1024], qt1[:],
                                wfh[:, 1024 * c:1024 * c + 1024],
                                op=OP.subtract)
            fence(v, i)
        i = v.tensor_copy(afh[:], afst[:])
        fence(v, i)
        i = v.tensor_tensor(afl[:], afst[:], afh[:], op=OP.subtract)
        fence(v, i)
        i = v.tensor_copy(bmh[:], bmst[:])
        fence(v, i)
        v.tensor_tensor(bml[:], bmst[:], bmh[:],
                        op=OP.subtract).then_inc(sV.h, 1)
    sV.n += 1
    quant_v = sV.n

    out_tiles = 0
    ep_v = [0] * 20
    rT_v_arr = [0] * 4
    ffn_t = [0] * 20
    sil_s = [0] * 20
    p2t_end = [0] * 4
    for mt in range(4):
        t0 = 512 * mt
        fmPh = fm2_h if mt % 2 == 0 else fm2B_h
        fmPl = fm2_l if mt % 2 == 0 else fm2B_l
        aorP = aor if mt % 2 == 0 else aorB

        @block.sync
        def _(sy, mt=mt, t0=t0, qv=quant_v, fmPh=fmPh, fmPl=fmPl, aorP=aorP):
            if mt == 0:
                sy.wait_ge(sV.h, qv)
            if mt == 1:
                sy.wait_ge(sV.h, qv)
            if mt >= 2:
                sy.wait_ge(sT.h, p2t_end[mt - 2])
            for c in range(8):
                sy.dma_start(fmPh[:, 512 * c:512 * c + 512],
                             xnh_s[c, :, t0:t0 + 512]).then_inc(dL.h, 16)
                sy.dma_start(fmPl[:, 512 * c:512 * c + 512],
                             xnl_s[c, :, t0:t0 + 512]).then_inc(dL.h, 16)
            for pgc in range(8):
                sy.dma_start(aorP[:, 512 * pgc:512 * pgc + 512],
                             aos[pgc, :, t0:t0 + 512]).then_inc(dL.h, 16)
            for tt in range(4):
                if mt >= 1:
                    sy.wait_ge(sV.h, ep_v[4 * (mt - 1) + tt])
                sy.dma_start(xres[tt][:],
                             xd[t0 + 128 * tt:t0 + 128 * tt + 128, :]
                             ).then_inc(dL.h, 16)
        dL.n += 28 * 16
        ld_d = dL.n
        sV_entry = sV.n
        sS_entry = sS.n
        rT_v_arr[mt] = sV_entry + 1
        for tt in range(4):
            ep_v[4 * mt + tt] = sV_entry + 3 + 2 * tt
            sil_s[4 * mt + tt] = sS_entry + tt + 1

        @block.tensor
        def _(t, mt=mt, ld_d=ld_d, quant_v=quant_v, fmPh=fmPh, fmPl=fmPl,
              aorP=aorP):
            fm2_h, fm2_l, aor = fmPh, fmPl, aorP
            t.wait_ge(dL.h, ld_d)
            t.wait_ge(sV.h, quant_v)
            if mt >= 1:
                t.wait_ge(sV.h, ep_v[4 * mt - 1])
            first = True
            last = None
            for c in range(8):
                for lh, rh in [(afh, fm2_h), (afh, fm2_l), (afl, fm2_h)]:
                    last = t.matmul(psU[0:16, :], lh[:, 16 * c:16 * c + 16],
                                    rh[:, 512 * c:512 * c + 512],
                                    start=first, stop=False,
                                    skip_group_check=True)
                    first = False
            last.then_inc(sT.h, 1)
            rT_t = sT.n + 1
            for tt in range(4):
                gt = 4 * mt + tt
                t.wait_ge(sV.h, rT_v_arr[mt])
                if gt >= 2:
                    t.wait_ge(sV.h, ep_v[gt - 2])
                ypb = psO if tt % 2 == 0 else psB
                last = None
                for fo in range(2):
                    first = True
                    for c in range(8):
                        for lh, rh in [(fm2_h, wfh), (fm2_l, wfh), (fm2_h, wfl)]:
                            last = t.matmul(
                                ypb[:, 512 * fo:512 * fo + 512],
                                lh[:, 512 * c + 128 * tt:512 * c + 128 * tt + 128],
                                rh[:, 1024 * c + 512 * fo:1024 * c + 512 * fo + 512],
                                start=first, stop=False, skip_group_check=True)
                            first = False
                    for lh, rh in [(rTh, bmh), (rTl, bmh), (rTh, bml)]:
                        last = t.matmul(ypb[:, 512 * fo:512 * fo + 512],
                                        lh[0:16, 128 * tt:128 * tt + 128],
                                        rh[:, 512 * fo:512 * fo + 512],
                                        start=False, stop=False,
                                        skip_group_check=True)
                for fo in range(2):
                    if tt % 2 == 0:
                        sout = psA[:, 512 * fo:512 * fo + 512]
                    else:
                        sout = (psU if fo == 0 else psV)[:, 0:512]
                    first = True
                    for pgc in range(8):
                        last = t.matmul(
                            sout,
                            aor[:, 512 * pgc + 128 * tt:512 * pgc + 128 * tt + 128],
                            woTs[:, 1024 * pgc + 512 * fo:1024 * pgc + 512 * fo + 512],
                            start=first, stop=False, skip_group_check=True)
                        first = False
                last.then_inc(sT.h, 1)
        sT.n += 5
        p2t_end[mt] = sT.n
        for tt in range(4):
            ffn_t[4 * mt + tt] = sT.n - 3 + tt

        @block.vector
        def _(v, mt=mt):
            v.wait_ge(sT.h, sT.n - 4)
            i = v.tensor_copy(rfb[:], psU[0:16, :])
            fence(v, i)
            i = v.tensor_copy(rTh[:], rfb[:])
            fence(v, i)
            v.tensor_tensor(rTl[:], rfb[:], rTh[:],
                            op=OP.subtract).then_inc(sV.h, 1)
        sV.n += 1
        assert rT_v_arr[mt] == sV.n

        for tt in range(4):
            gt = 4 * mt + tt
            ypb = psO if tt % 2 == 0 else psB

            @block.vector
            def _(v, tt=tt, gt=gt, ypb=ypb):
                v.wait_ge(sT.h, ffn_t[gt])
                v.tensor_tensor(ybuf[:], ypb[:], fb_b[:],
                                op=OP.add).then_inc(sV.h, 1)
            sV.n += 1

            @block.scalar
            def _(s, gt=gt):
                s.wait_ge(sV.h, sV.n)
                s.activation(gbuf[:], ybuf[:], AF.Silu, bias=0.0,
                             scale=1.0).then_inc(sS.h, 1)
            sS.n += 1
            assert sil_s[gt] == sS.n

            @block.vector
            def _(v, tt=tt, gt=gt, mt=mt):
                v.wait_ge(sS.h, sil_s[gt])
                if tt % 2 == 0:
                    i = v.tensor_tensor(sbuf2[:], psA[:], bo_b[:], op=OP.add)
                else:
                    v.tensor_tensor(sbuf2[:, 0:512], psU[:], bo_b[:, 0:512],
                                    op=OP.add)
                    i = v.tensor_tensor(sbuf2[:, 512:1024], psV[:],
                                        bo_b[:, 512:1024], op=OP.add)
                fence(v, i)
                i = v.tensor_tensor(sbuf2[:], sbuf2[:], gbuf[:], op=OP.mult)
                fence(v, i)
                if gt >= 2:
                    v.wait_ge(dO.h, 16 * (gt - 1))
                v.tensor_tensor(outst[tt % 2][:], xres[tt][:], sbuf2[:],
                                op=OP.add).then_inc(sV.h, 1)
            sV.n += 1
            assert ep_v[gt] == sV.n

            @block.sync
            def _(sy, tt=tt, t0=t0, gt=gt):
                sy.wait_ge(sV.h, ep_v[gt])
                sy.dma_start(yd[t0 + 128 * tt:t0 + 128 * tt + 128, :],
                             outst[tt % 2][:]).then_inc(dO.h, 16)
            out_tiles += 1

    @block.sync
    def _(sy):
        sy.wait_ge(dO.h, 16 * out_tiles)

    nc.leave_named_scope("p2", _sid_p2, False)
    st_p2.close()
    st_all.close()
    return nc


_NC_CACHE = [None]


def kernel(L0_attn_x, L0_attn_Wq, L0_attn_bq, L0_attn_Wk, L0_attn_bk,
           L0_attn_Wv, L0_attn_bv, L0_attn_Wo, L0_attn_bo,
           L0_attn_prior_lambda, L0_ffn_kernel, L0_ffn_bias, L0_ffn_a,
           L0_ffn_b):
    x = np.ascontiguousarray(np.asarray(L0_attn_x, dtype=np.float32))
    wqT = np.ascontiguousarray(np.asarray(L0_attn_Wq, np.float32).T)
    wkT = np.ascontiguousarray(np.asarray(L0_attn_Wk, np.float32).T)
    wvT = np.ascontiguousarray(np.asarray(L0_attn_Wv, np.float32).T)
    woT = np.ascontiguousarray(np.asarray(L0_attn_Wo, np.float32).T)
    fk = np.ascontiguousarray(np.asarray(L0_ffn_kernel, np.float32))
    fa = np.ascontiguousarray(np.asarray(L0_ffn_a, np.float32))
    fbmat = np.ascontiguousarray(np.asarray(L0_ffn_b, np.float32))
    lam = np.asarray(L0_attn_prior_lambda, np.float32).reshape(1, 1)

    p = np.arange(128, dtype=np.int64)[:, None]
    j = np.arange(384, dtype=np.int64)[None, :]
    delta_real = (j - p - 128).astype(np.float32)
    delta_far = (j - p + 3000).astype(np.float32)
    e2m = np.zeros((2, 128), dtype=ml_dtypes.bfloat16)
    e2m[0, 0:64] = 1
    e2m[1, 64:128] = 1
    identf = np.eye(128, dtype=np.float32)
    identb = np.eye(128, dtype=ml_dtypes.bfloat16)

    if _NC_CACHE[0] is None:
        _NC_CACHE[0] = build()
    nc = _NC_CACHE[0]

    in_maps = []
    for c in range(8):
        b, half = c // 2, c % 2
        xb = np.ascontiguousarray(np.roll(x[b], -TMY * half, axis=0))
        in_maps.append({
            "x": xb, "wqT": wqT, "wkT": wkT, "wvT": wvT, "woT": woT,
            "bq": np.asarray(L0_attn_bq, np.float32),
            "bk": np.asarray(L0_attn_bk, np.float32),
            "bv": np.asarray(L0_attn_bv, np.float32),
            "bo": np.asarray(L0_attn_bo, np.float32),
            "lam": lam, "fk": fk,
            "fb": np.asarray(L0_ffn_bias, np.float32),
            "fa": fa, "fbm": fbmat,
            "dlc": delta_real,
            "dl0": delta_real if half == 0 else delta_far,
            "dl1": delta_real if half == 1 else delta_far,
            "e2": e2m, "identf": identf, "identb": identb,
        })
    res = run_bass_kernel_spmd(nc, in_maps, core_ids=list(range(8)))
    out = np.empty((4, L, D), dtype=np.float32)
    for c in range(8):
        b, half = c // 2, c % 2
        out[b, TMY * half:TMY * half + TMY, :] = res.results[c]["y"]
    return out
